# version 21
# baseline (speedup 1.0000x reference)
import math
from contextlib import ExitStack

import numpy as np
import concourse.bass as bass
import concourse.mybir as mybir
from concourse.bass_utils import run_bass_kernel_spmd
from concourse.expressions_rust import RuntimeValue

F32 = mybir.dt.float32
BF16 = mybir.dt.bfloat16
I32 = mybir.dt.int32
AF = mybir.ActivationFunctionType
ALU = mybir.AluOpType
ds = bass.ds
X = mybir.AxisListType.X

D = 4096
TS = 2048
OWN = 1024
NH = 32
SCALE = 1.0 / math.sqrt(128.0)
EPS = 1e-5

ENGS = ("sync", "scalar", "vector", "tensor", "gpsimd")


class Res:
    __slots__ = ("w", "rs")

    def __init__(self):
        self.w = None
        self.rs = []


NLANE = 8
LANES = [f"{e}D{i}" for e in ("sync", "gpsimd", "scalar") for i in range(NLANE)]


class Plan:

    def __init__(self):
        self.ops = {e: [] for e in ENGS}
        self.cnt = {e: 0 for e in ENGS}
        for l in LANES:
            self.cnt[l] = 0
        self.dma_i = {e: 0 for e in ENGS}

    def op(self, eng, fn, reads=(), writes=(), dma=False, extra_waits=()):
        waits = {}

        def add(ec):
            if ec is None:
                return
            e, c = ec
            if e == eng == "tensor":
                return
            if waits.get(e, 0) < c:
                waits[e] = c
        for r in reads:
            add(r.w)
        for r in writes:
            add(r.w)
            for rd in r.rs:
                add(rd)
        for ec in extra_waits:
            add(ec)
        if dma:
            lane = f"{eng}D{self.dma_i[eng] % NLANE}"
            self.dma_i[eng] += 1
            if self.cnt[lane] > waits.get(lane, 0):
                waits[lane] = self.cnt[lane]
            self.cnt[lane] += 16
            me = (lane, self.cnt[lane])
            key = lane
        else:
            self.cnt[eng] += 1
            me = (eng, self.cnt[eng])
            key = eng
        for r in reads:
            r.rs.append(me)
        for r in writes:
            r.w = me
            r.rs = []
        self.ops[eng].append((fn, sorted(waits.items()), key))

    def raw(self, eng, fn):
        self.ops[eng].append((fn, [], None))

    def barrier(self):
        snap = {k: v for k, v in self.cnt.items() if v > 0}
        for e in ENGS:
            waits = [(o, c) for o, c in snap.items() if o != e]
            self.ops[e].append(("waits_only", waits, None))


def build_program():
    nc = bass.Bass("TRN2", target_bir_lowering=False)

    x = nc.dram_tensor("x", [TS, D], F32, kind="ExternalInput").ap()
    w1 = nc.dram_tensor("w1", [6144, D], F32, kind="ExternalInput").ap()
    b1 = nc.dram_tensor("b1", [6144], F32, kind="ExternalInput").ap()
    w2 = nc.dram_tensor("w2", [D, D], F32, kind="ExternalInput").ap()
    b2 = nc.dram_tensor("b2", [D], F32, kind="ExternalInput").ap()
    wf = nc.dram_tensor("wf", [D, D], F32, kind="ExternalInput").ap()
    bf = nc.dram_tensor("bf", [D], F32, kind="ExternalInput").ap()
    la = nc.dram_tensor("la", [64, D], F32, kind="ExternalInput").ap()
    lb = nc.dram_tensor("lb", [D, 64], F32, kind="ExternalInput").ap()
    gw = nc.dram_tensor("gw", [4, D], F32, kind="ExternalInput").ap()
    gb = nc.dram_tensor("gb", [4], F32, kind="ExternalInput").ap()
    sched = nc.dram_tensor("sched", [1, 2], F32, kind="ExternalInput").ap()
    y = nc.dram_tensor("y", [OWN, D], F32, kind="ExternalOutput").ap()

    kspill = nc.dram_tensor("kspill", [2, 128, 8, TS], BF16, kind="Internal").ap()
    vspill = nc.dram_tensor("vspill", [128, 16, NH * 33], F32, kind="Internal").ap()
    y1scr = nc.dram_tensor("y1scr", [8, 128, D], F32, kind="Internal").ap()
    wt1_scr = nc.dram_tensor("wt1_scr", [2, 128, 24, D], BF16, kind="Internal").ap()
    wtf_scr = nc.dram_tensor("wtf_scr", [2, 128, 32, D], BF16, kind="Internal").ap()
    wt2_scr = nc.dram_tensor("wt2_scr", [2, 128, 8, D], BF16, kind="Internal").ap()


    t01_np = np.zeros((128, 896), dtype=np.float32)
    for pp in range(128):
        t01_np[pp, 384 + pp:] = 1.0
    t01_h = nc.inline_tensor(t01_np, name="t01c")
    id_h = nc.inline_tensor(np.eye(128, dtype=np.float32), name="idc")
    iota_h = nc.inline_tensor(
        np.ascontiguousarray(np.broadcast_to(np.arange(64, dtype=np.float32), (128, 64))),
        name="iotac")
    r128_np = np.zeros((128, 32), dtype=np.float32)
    for pp in range(128):
        r128_np[pp, pp // 4] = 1.0
    r128_h = nc.inline_tensor(r128_np, name="r128c")
    h4_np = np.zeros((128, 4), dtype=np.float32)
    for pp in range(128):
        h4_np[pp, pp // 32] = 1.0
    h4_h = nc.inline_tensor(h4_np, name="h4c")

    P = Plan()
    top = ExitStack()
    sems = {e: top.enter_context(nc.semaphore(f"sm_{e}")) for e in ENGS}
    for l in LANES:
        sems[l] = top.enter_context(nc.semaphore(f"sm_{l}"))

    def sb_in(stack, name, shape, dtype=F32):
        return stack.enter_context(nc.sbuf_tensor(name, shape, dtype))

    t01 = sb_in(top, "t01", [128, 896])
    id128 = sb_in(top, "id128", [128, 128])
    iota64 = sb_in(top, "iota64", [128, 64])
    r128 = sb_in(top, "r128", [128, 32])
    h4 = sb_in(top, "h4", [128, 4])
    ones1 = sb_in(top, "ones1", [1, 128])
    zrow = sb_in(top, "zrow", [1, 512])
    ssb = sb_in(top, "ssb", [1, 2])
    gmbc = sb_in(top, "gmbc", [128, 2])
    b1f = sb_in(top, "b1f", [128, 24])
    b1q4 = sb_in(top, "b1q4", [128, 8, 4])
    bsb = sb_in(top, "bsb", [128, 32])
    gbsb = sb_in(top, "gbsb", [1, 4])
    gbbc = sb_in(top, "gbbc", [128, 4])
    negm = sb_in(top, "negm", [128, 32])
    nqrow = sb_in(top, "nqrow", [1, 32])
    nkrow = sb_in(top, "nkrow", [1, 32])
    nk48 = sb_in(top, "nk48", [4, 8])
    nk4t = sb_in(top, "nk4t", [4, 1])
    ln2m = sb_in(top, "ln2m", [128, 8])
    epsc = sb_in(top, "epsc", [128, 1])
    sclc = sb_in(top, "sclc", [128, 1])

    pb = [top.enter_context(nc.psum_tensor(f"pb{i}", [128, 512], F32)) for i in range(6)]
    pa = top.enter_context(nc.psum_tensor("pa", [128, 1024], F32))
    R_pb = [Res() for _ in range(6)]
    R_pa = [Res(), Res()]

    rs = {}

    def res(name):
        if name not in rs:
            rs[name] = Res()
        return rs[name]

    def dma(eng, out_ap, in_ap, reads=(), writes=(), slow=False):
        if slow:
            P.op(eng, lambda e, o=out_ap, i=in_ap: e.dma_start(
                out=o, in_=i, allow_slow_non_contiguous=True),
                reads=reads, writes=writes, dma=True)
        else:
            P.op(eng, lambda e, o=out_ap, i=in_ap: e.dma_start(out=o, in_=i),
                 reads=reads, writes=writes, dma=True)

    dma("sync", t01[:], t01_h.ap(), writes=[res("t01")])
    dma("sync", id128[:], id_h.ap(), writes=[res("id128")])
    dma("sync", iota64[:], iota_h.ap(), writes=[res("iota64")])
    dma("sync", r128[:], r128_h.ap(), writes=[res("r128")])
    dma("sync", h4[:], h4_h.ap(), writes=[res("h4")])
    dma("sync", ssb[:], sched, writes=[res("ssb")])
    P.op("vector", lambda e: e.memset(ones1[:], 1.0), writes=[res("ones1")])
    P.op("vector", lambda e: e.memset(epsc[:], float(EPS)), writes=[res("epsc")])
    P.op("vector", lambda e: e.memset(sclc[:], float(SCALE)), writes=[res("sclc")])
    nc.const_aps.register(F32, float(EPS), epsc[:])
    nc.const_aps.register(F32, float(SCALE), sclc[:])
    P.op("vector", lambda e: e.memset(zrow[:], 0.0), writes=[res("zrow")])
    dma("sync", b1q4[:], bass.AP(b1.tensor, 0, [[4, 128], [512, 8], [1, 4]]),
        writes=[res("b1q4")])
    P.op("vector", lambda e: e.reduce_sum(b1f[:, 0:8].unsqueeze(2), b1q4[:], axis=X),
         reads=[res("b1q4")], writes=[res("b1f")])
    dma("sync", b1f[:, 8:24], bass.AP(b1.tensor, 4096, [[1, 128], [128, 16]]),
        writes=[res("b1f")], slow=True)
    dma("sync", bsb[:], bass.AP(bf.tensor, 0, [[1, 128], [128, 32]]),
        writes=[res("bsb")], slow=True)
    dma("sync", gbsb[:], gb[None, :], writes=[res("gbsb")])
    P.op("tensor", lambda e: e.matmul(pb[0][:, 0:4], ones1[:], gbsb[:], start=True, stop=True),
         reads=[res("ones1"), res("gbsb")], writes=[R_pb[0]])
    P.op("vector", lambda e: e.tensor_copy(gbbc[:], pb[0][:, 0:4]),
         reads=[R_pb[0]], writes=[res("gbbc")])
    P.op("tensor", lambda e: e.matmul(pb[0][:, 0:2], ones1[:], ssb[:], start=True, stop=True),
         reads=[res("ones1"), res("ssb")], writes=[R_pb[0]])
    P.op("vector", lambda e: e.tensor_copy(gmbc[:], pb[0][:, 0:2]),
         reads=[R_pb[0]], writes=[res("gmbc")])

    with ExitStack() as ph:
        w1slab = [sb_in(ph, "w1slabA", [128, D]), sb_in(ph, "w1slabB", [128, D])]
        qstage = sb_in(ph, "qstage", [128, D])
        othq = [sb_in(ph, "othqA", [128, D], BF16), sb_in(ph, "othqB", [128, D], BF16)]
        otlq = [sb_in(ph, "otlqA", [128, D], BF16), sb_in(ph, "otlqB", [128, D], BF16)]
        w2slab = [sb_in(ph, "w2slabA", [128, D]), sb_in(ph, "w2slabB", [128, D])]
        w2fold = [sb_in(ph, "w2foldA", [128, 1024]), sb_in(ph, "w2foldB", [128, 1024])]
        oh2 = [sb_in(ph, "oh2A", [128, 1024], BF16), sb_in(ph, "oh2B", [128, 1024], BF16)]
        ol2 = [sb_in(ph, "ol2A", [128, 1024], BF16), sb_in(ph, "ol2B", [128, 1024], BF16)]
        for sl in range(32):
            bb = sl % 2
            dma("sync", w1slab[bb][:], w1[sl * 128:(sl + 1) * 128, :],
                reads=[], writes=[res(f"w1slab{bb}")])
            q4 = sl % 4
            for j in range(8):
                P.op("tensor",
                     lambda e, j=j, bb=bb: e.matmul(pb[j % 2][0:32, :], r128[:],
                                                    w1slab[bb][:, j * 512:(j + 1) * 512],
                                                    start=True, stop=True),
                     reads=[res("r128"), res(f"w1slab{bb}")], writes=[R_pb[j % 2]])
                P.op("vector",
                     lambda e, j=j, q4=q4: e.tensor_copy(
                         qstage[q4 * 32:(q4 + 1) * 32, j * 512:(j + 1) * 512],
                         pb[j % 2][0:32, :]),
                     reads=[R_pb[j % 2]], writes=[res(f"qst_{q4}_{j}")])
            if q4 == 3:
                oc = sl // 4
                qb_ = oc % 2
                for dc in range(32):
                    tb = 2 + ((oc * 32 + dc) % 4)
                    P.op("tensor", lambda e, dc=dc, b=tb: e.transpose(
                        pb[b][:, 0:128], qstage[:, dc * 128:(dc + 1) * 128], id128[:]),
                        reads=[res(f"qst_{q_}_{dc // 4}") for q_ in range(4)]
                              + [res("id128")],
                        writes=[R_pb[tb]])
                    P.op("scalar", lambda e, dc=dc, b=tb, qb_=qb_: e.activation(
                        othq[qb_][:, dc * 128:(dc + 1) * 128], pb[b][:, 0:128],
                        AF.Copy, bias=0.0, scale=1.0),
                        reads=[R_pb[tb]], writes=[res(f"othq{qb_}_{dc}")])
                    P.op("vector", lambda e, dc=dc, b=tb, qb_=qb_: e.scalar_tensor_tensor(
                        otlq[qb_][:, dc * 128:(dc + 1) * 128], pb[b][:, 0:128], 1.0,
                        othq[qb_][:, dc * 128:(dc + 1) * 128],
                        op0=ALU.mult, op1=ALU.subtract),
                        reads=[R_pb[tb], res(f"othq{qb_}_{dc}")],
                        writes=[res(f"otlq{qb_}_{dc}")])
                for h2_, ob, pre in ((0, othq, "othq"), (1, otlq, "otlq")):
                    dma("gpsimd",
                        bass.AP(wt1_scr.tensor, h2_ * 128 * 24 * D + oc * D,
                                [[24 * D, 128], [1, D]]),
                        ob[qb_][:],
                        reads=[res(f"{pre}{qb_}_{dc}") for dc in range(32)],
                        writes=[res(f"wtoq_{oc}_{h2_}")])

            dma("sync", w2slab[bb][:], w2[sl * 128:(sl + 1) * 128, :],
                reads=[], writes=[res(f"w2slab{bb}")])
            w2r = w2slab[bb][:].rearrange("p (c r) -> p c r", r=4)
            P.op("vector", lambda e, w2r=w2r, bb=bb: e.tensor_add(
                w2fold[bb][:].unsqueeze(2), w2r[:, :, 0:1], w2r[:, :, 1:2]),
                reads=[res(f"w2slab{bb}")], writes=[res(f"w2fold{bb}")])
            P.op("vector", lambda e, w2r=w2r, bb=bb: e.tensor_add(
                w2fold[bb][:].unsqueeze(2), w2fold[bb][:].unsqueeze(2), w2r[:, :, 2:3]),
                reads=[res(f"w2slab{bb}")], writes=[res(f"w2fold{bb}")])
            P.op("vector", lambda e, w2r=w2r, bb=bb: e.tensor_add(
                w2fold[bb][:].unsqueeze(2), w2fold[bb][:].unsqueeze(2), w2r[:, :, 3:4]),
                reads=[res(f"w2slab{bb}")], writes=[res(f"w2fold{bb}")])
            for cc in range(8):
                tb = 2 + (cc % 4)
                P.op("tensor", lambda e, cc=cc, bb=bb, b=tb: e.transpose(
                    pb[b][:, 0:128], w2fold[bb][:, cc * 128:(cc + 1) * 128], id128[:]),
                    reads=[res(f"w2fold{bb}"), res("id128")], writes=[R_pb[tb]])
                P.op("scalar", lambda e, cc=cc, bb=bb, b=tb: e.activation(
                    oh2[bb][:, cc * 128:(cc + 1) * 128], pb[b][:, 0:128],
                    AF.Copy, bias=0.0, scale=1.0),
                    reads=[R_pb[tb]], writes=[res(f"oh2{bb}_{cc}")])
                P.op("vector", lambda e, cc=cc, bb=bb, b=tb: e.scalar_tensor_tensor(
                    ol2[bb][:, cc * 128:(cc + 1) * 128], pb[b][:, 0:128], 1.0,
                    oh2[bb][:, cc * 128:(cc + 1) * 128],
                    op0=ALU.mult, op1=ALU.subtract),
                    reads=[R_pb[tb], res(f"oh2{bb}_{cc}")], writes=[res(f"ol2{bb}_{cc}")])
            for h2_, ob, rn in ((0, oh2, "oh2"), (1, ol2, "ol2")):
                dma("gpsimd",
                    bass.AP(wt2_scr.tensor, h2_ * 128 * 8 * D + sl * 128,
                            [[8 * D, 128], [D, 8], [1, 128]]),
                    ob[bb][:].rearrange("p (a b) -> p a b", b=128),
                    reads=[res(f"{rn}{bb}_{cc}") for cc in range(8)],
                    writes=[res(f"wt2o_{sl}_{h2_}")])

        with ExitStack() as pp:
            psl = [sb_in(pp, "pslA", [128, D]), sb_in(pp, "pslB", [128, D])]
            oth = [sb_in(pp, "othA", [128, D], BF16), sb_in(pp, "othB", [128, D], BF16)]
            otl = [sb_in(pp, "otlA", [128, D], BF16), sb_in(pp, "otlB", [128, D], BF16)]

            jobs = []
            for oc in range(8, 24):
                jobs.append((w1[4096 + (oc - 8) * 128: 4096 + (oc - 7) * 128, :],
                             [], wt1_scr, oc, 24))
            for oc in range(32):
                jobs.append((wf[oc * 128:(oc + 1) * 128, :], [], wtf_scr, oc, 32))

            for ji, (srcap, rd, dscr, oc, noc) in enumerate(jobs):
                jb = ji % 2
                dma("sync", psl[jb][:], srcap,
                    reads=rd, writes=[res(f"psl{jb}")])
                for dc in range(32):
                    tb = (ji * 32 + dc) % 6
                    P.op("tensor", lambda e, dc=dc, jb=jb, b=tb: e.transpose(
                        pb[b][:, 0:128], psl[jb][:, dc * 128:(dc + 1) * 128], id128[:]),
                        reads=[res(f"psl{jb}"), res("id128")], writes=[R_pb[tb]])
                    P.op("scalar", lambda e, dc=dc, jb=jb, b=tb: e.activation(
                        oth[jb][:, dc * 128:(dc + 1) * 128], pb[b][:, 0:128],
                        AF.Copy, bias=0.0, scale=1.0),
                        reads=[R_pb[tb]], writes=[res(f"oth{jb}_{dc}")])
                    P.op("vector", lambda e, dc=dc, jb=jb, b=tb: e.scalar_tensor_tensor(
                        otl[jb][:, dc * 128:(dc + 1) * 128], pb[b][:, 0:128], 1.0,
                        oth[jb][:, dc * 128:(dc + 1) * 128],
                        op0=ALU.mult, op1=ALU.subtract),
                        reads=[R_pb[tb], res(f"oth{jb}_{dc}")], writes=[res(f"otl{jb}_{dc}")])
                for h2_, pre in ((0, "oth"), (1, "otl")):
                    dma("gpsimd",
                        bass.AP(dscr.tensor, h2_ * 128 * noc * D + oc * D,
                                [[noc * D, 128], [1, D]]),
                        (oth if h2_ == 0 else otl)[jb][:],
                        reads=[res(f"{pre}{jb}_{dc}") for dc in range(32)],
                        writes=[res(f"wto_{ji}_{h2_}")])
    P.barrier()

    st_qo = ExitStack()
    qbh = sb_in(st_qo, "qbh", [128, 8, OWN], BF16)
    qbl = sb_in(st_qo, "qbl", [128, 8, OWN], BF16)

    with ExitStack() as ph:
        xsb = [sb_in(ph, "xsbA", [128, D]), sb_in(ph, "xsbB", [128, D])]
        lnxTh = sb_in(ph, "lnxTh", [128, 32, 512], BF16)
        lnxTl = sb_in(ph, "lnxTl", [128, 32, 512], BF16)
        stat = [sb_in(ph, "statA", [128, 6]), sb_in(ph, "statB", [128, 6])]
        wtTh = [sb_in(ph, "wtThA", [128, 2048], BF16), sb_in(ph, "wtThB", [128, 2048], BF16)]
        wtTl = [sb_in(ph, "wtTlA", [128, 2048], BF16), sb_in(ph, "wtTlB", [128, 2048], BF16)]
        kth = sb_in(ph, "kth", [128, 8, 512], BF16)
        ktl = sb_in(ph, "ktl", [128, 8, 512], BF16)
        vstage = sb_in(ph, "vstage", [128, 512])
        vtile = sb_in(ph, "vtile", [128, 4, NH * 33])
        kflat = kth[:].rearrange("p a b -> p (a b)")

        for tt in range(4):
            for sl in range(4):
                xb_ = (tt * 4 + sl) % 2
                dma("sync", xsb[xb_][:], x[tt * 512 + sl * 128: tt * 512 + (sl + 1) * 128, :],
                    reads=[], writes=[res(f"xsb{xb_}")])
                st_ = stat[xb_]
                sk = f"stat{xb_}"
                P.op("vector", lambda e, xb_=xb_, st_=st_: e.reduce_sum(
                    st_[:, 0:1], xsb[xb_][:], axis=X),
                     reads=[res(f"xsb{xb_}")], writes=[res(sk)])
                P.op("scalar", lambda e, xb_=xb_, st_=st_: e.activation(
                    kflat[:, 0:4096], xsb[xb_][:],
                    AF.Square, bias=0.0, scale=1.0, accum_out=st_[:, 1:2]),
                     reads=[res(f"xsb{xb_}")],
                     writes=[res(sk)] + [res(f"kth_{g}") for g in range(8)])
                P.op("vector", lambda e, st_=st_: e.tensor_scalar(st_[:, 0:1], st_[:, 0:1],
                                                         1.0 / D, None, op0=ALU.mult),
                     reads=[], writes=[res(sk)])
                P.op("vector", lambda e, st_=st_: e.tensor_tensor(st_[:, 5:6], st_[:, 0:1],
                                                         st_[:, 0:1], op=ALU.mult),
                     reads=[], writes=[res(sk)])
                P.op("vector", lambda e, st_=st_: e.scalar_tensor_tensor(
                    st_[:, 2:3], st_[:, 1:2], 1.0 / D, st_[:, 5:6],
                    op0=ALU.mult, op1=ALU.subtract),
                    reads=[], writes=[res(sk)])
                P.op("scalar", lambda e, st_=st_: e.activation(st_[:, 3:4], st_[:, 2:3], AF.Sqrt,
                                                      bias=float(EPS), scale=1.0),
                     reads=[res(sk)], writes=[res(sk)])
                P.op("vector", lambda e, st_=st_: e.reciprocal(st_[:, 3:4], st_[:, 3:4]),
                     reads=[], writes=[res(sk)])
                P.op("vector", lambda e, st_=st_: e.tensor_tensor(st_[:, 4:5], st_[:, 0:1],
                                                         st_[:, 3:4], op=ALU.mult),
                     reads=[], writes=[res(sk)])
                P.op("vector", lambda e, st_=st_: e.tensor_scalar(st_[:, 4:5], st_[:, 4:5],
                                                         -1.0, None, op0=ALU.mult),
                     reads=[], writes=[res(sk)])
                P.op("scalar", lambda e, xb_=xb_, st_=st_: e.activation(
                    xsb[xb_][:], xsb[xb_][:], AF.Identity,
                    bias=st_[:, 4:5], scale=st_[:, 3:4]),
                     reads=[res(sk)], writes=[res(f"xsb{xb_}")])

                for dc in range(32):
                    bk = dc % 4
                    P.op("tensor", lambda e, dc=dc, b=bk, xb_=xb_: e.transpose(
                        pb[b][:, 0:128], xsb[xb_][:, dc * 128:(dc + 1) * 128], id128[:]),
                        reads=[res(f"xsb{xb_}"), res("id128")], writes=[R_pb[bk]])
                    P.op("scalar", lambda e, dc=dc, sl=sl, b=bk: e.activation(
                        lnxTh[:, dc, sl * 128:(sl + 1) * 128], pb[b][:, 0:128],
                        AF.Copy, bias=0.0, scale=1.0),
                        reads=[R_pb[bk]], writes=[res(f"lnxTh_{dc}_{sl}")])
                    P.op("vector", lambda e, dc=dc, sl=sl, b=bk: e.scalar_tensor_tensor(
                        lnxTl[:, dc, sl * 128:(sl + 1) * 128], pb[b][:, 0:128], 1.0,
                        lnxTh[:, dc, sl * 128:(sl + 1) * 128],
                        op0=ALU.mult, op1=ALU.subtract),
                        reads=[R_pb[bk], res(f"lnxTh_{dc}_{sl}")], writes=[res(f"lnxTl_{dc}_{sl}")])
            ocs = list(range(24)) if tt < 2 else list(range(8, 24))

            def wt1ap(oc, h2_, hh):
                return bass.AP(wt1_scr.tensor, h2_ * 128 * 24 * D + oc * D + hh * 2048,
                               [[24 * D, 128], [1, 2048]])

            def fetch_half(j, halves):
                oc, hh = halves[j]
                b = j % 2
                dma("sync", wtTh[b][:], wt1ap(oc, 0, hh),
                    reads=[], writes=[res(f"wtTh{b}")])
                dma("sync", wtTl[b][:], wt1ap(oc, 1, hh),
                    reads=[], writes=[res(f"wtTl{b}")])

            def mm1(j, dc):
                b = j % 2
                c0 = (dc % 16) * 128
                for pi, (lw, rx, rl, rr_) in enumerate((
                        (wtTh, lnxTh, f"wtTh{b}", "lnxTh"),
                        (wtTh, lnxTl, f"wtTh{b}", "lnxTl"),
                        (wtTl, lnxTh, f"wtTl{b}", "lnxTh"))):
                    P.op("tensor", lambda e, dc=dc, b=b, c0=c0, lw=lw, rx=rx: e.matmul(
                        pb[4][:, 0:512], lw[b][:, c0:c0 + 128], rx[:, dc, :],
                        start=(dc == 0 and lw is wtTh and rx is lnxTh),
                        stop=(dc == 31 and lw is wtTl)),
                        reads=[res(rl)] + [res(f"{rr_}_{dc}_{s_}") for s_ in range(4)],
                        writes=[R_pb[4]])

            def evac_oc(oc, tt):
                if oc < 8:
                    hid = qbh[:, oc, tt * 512:(tt + 1) * 512]
                    lod = qbl[:, oc, tt * 512:(tt + 1) * 512]
                    hr, lr = f"qbh_{oc}_{tt}", f"qbl_{oc}_{tt}"
                elif oc < 16:
                    hid = kth[:, oc - 8, :]
                    lod = ktl[:, oc - 8, :]
                    hr, lr = f"kth_{oc - 8}", f"ktl_{oc - 8}"
                else:
                    hid = lod = None
                if hid is not None:
                    P.op("vector", lambda e, oc=oc, hid=hid: e.tensor_scalar(
                        hid, pb[4][:, 0:512], b1f[:, oc:oc + 1], None, op0=ALU.add),
                        reads=[R_pb[4], res("b1f")], writes=[res(hr)])
                    P.op("vector", lambda e, oc=oc, hid=hid, lod=lod: e.scalar_tensor_tensor(
                        lod, pb[4][:, 0:512], b1f[:, oc:oc + 1], hid,
                        op0=ALU.add, op1=ALU.subtract),
                        reads=[R_pb[4], res("b1f"), res(hr)], writes=[res(lr)])
                    if 8 <= oc < 16:
                        g = oc - 8
                        P.op("vector", lambda e, oc=oc, hid=hid: e.scalar_tensor_tensor(
                            vstage[:], pb[4][:, 0:512], b1f[:, oc:oc + 1], hid,
                            op0=ALU.add, op1=ALU.mult),
                            reads=[R_pb[4], res("b1f"), res(hr)], writes=[res("vstage")])
                        P.op("tensor", lambda e: e.matmul(pb[5][0:4, 0:512], h4[:], vstage[:],
                                                          start=True, stop=True),
                             reads=[res("h4"), res("vstage")], writes=[R_pb[5]])
                        P.op("vector", lambda e: e.reduce_max(nk4t[0:4, 0:1],
                                                              pb[5][0:4, 0:512], axis=X),
                             reads=[R_pb[5]], writes=[res("nk4t")])
                        if tt == 0:
                            P.op("vector", lambda e, g=g: e.tensor_copy(
                                nk48[0:4, g:g + 1], nk4t[0:4, 0:1]),
                                reads=[res("nk4t")], writes=[res("nk48")])
                        else:
                            P.op("vector", lambda e, g=g: e.tensor_tensor(
                                nk48[0:4, g:g + 1], nk48[0:4, g:g + 1], nk4t[0:4, 0:1],
                                op=ALU.max),
                                reads=[res("nk4t")], writes=[res("nk48")])
                else:
                    g = oc - 16
                    P.op("vector", lambda e, oc=oc: e.tensor_scalar(
                        vstage[:], pb[4][:, 0:512], b1f[:, oc:oc + 1], None, op0=ALU.add),
                        reads=[R_pb[4], res("b1f")], writes=[res("vstage")])
                    for sub in range(4):
                        P.op("tensor", lambda e, sub=sub: e.transpose(
                            pb[5][:, 0:128], vstage[:, sub * 128:(sub + 1) * 128], id128[:]),
                            reads=[res("vstage"), res("id128")], writes=[R_pb[5]])
                        dstv = vtile[:].rearrange("p a (h ee) -> p a h ee", ee=33)[
                            :, sub, 4 * g:4 * g + 4, 0:32]
                        srcv = pb[5][:, 0:128].rearrange("p (h ee) -> p h ee", ee=32)
                        P.op("vector", lambda e, d=dstv, s=srcv: e.tensor_copy(d, s),
                             reads=[R_pb[5]], writes=[res(f"vtile_{g}_{sub}")])

            halves = [(oc, hh) for oc in ocs for hh in (0, 1)]
            fetch_half(0, halves)
            fetch_half(1, halves)
            steps = [(oc, dc) for oc in ocs for dc in range(32)]
            for si, (oc, dc) in enumerate(steps):
                mm1(si // 16, dc)
                if dc % 16 == 15 and si // 16 + 2 < len(halves):
                    fetch_half(si // 16 + 2, halves)
                if dc == 31:
                    evac_oc(oc, tt)
            vones = vtile[:].rearrange("p a (h ee) -> p a h ee", ee=33)[:, :, :, 32:33]
            P.op("vector", lambda e, v=vones: e.memset(v, 1.0), writes=[res("vones")])
            KSP = 128 * 8 * TS
            dma("gpsimd", bass.AP(kspill.tensor, tt * 512, [[8 * TS, 128], [TS, 8], [1, 512]]),
                kth[:], reads=[res(f"kth_{g}") for g in range(8)],
                writes=[res(f"kspill_h_{tt}")])
            dma("gpsimd", bass.AP(kspill.tensor, KSP + tt * 512, [[8 * TS, 128], [TS, 8], [1, 512]]),
                ktl[:], reads=[res(f"ktl_{g}") for g in range(8)],
                writes=[res(f"kspill_l_{tt}")])
            dma("gpsimd",
                bass.AP(vspill.tensor, tt * 4 * (NH * 33), [[16 * NH * 33, 128], [1, 4 * NH * 33]]),
                vtile[:].rearrange("p a b -> p (a b)"),
                reads=[res(f"vtile_{g}_{sub}") for g in range(8) for sub in range(4)]
                      + [res("vones")],
                writes=[res(f"vspill_{tt}")])
        for g in range(8):
            for t4 in range(2):
                P.op("vector", lambda e, g=g, t=t4: e.tensor_tensor(
                    vstage[:], qbh[:, g, t * 512:(t + 1) * 512],
                    qbl[:, g, t * 512:(t + 1) * 512], op=ALU.add),
                    reads=[res(f"qbh_{g}_{t4}"), res(f"qbl_{g}_{t4}")], writes=[res("vstage")])
                P.op("vector", lambda e: e.tensor_tensor(
                    vstage[:], vstage[:], vstage[:], op=ALU.mult),
                    reads=[], writes=[res("vstage")])
                P.op("tensor", lambda e, t=t4: e.matmul(
                    pa[0:4, t * 512:(t + 1) * 512], h4[:], vstage[:],
                    start=True, stop=True),
                    reads=[res("h4"), res("vstage")], writes=[R_pa[t4]])
            P.op("vector", lambda e: e.reduce_max(vstage[0:4, 0:1], pa[0:4, 0:1024], axis=X),
                 reads=[R_pa[0], R_pa[1]], writes=[res("vstage")])
            P.op("tensor", lambda e: e.transpose(pb[5][0:1, 0:4], vstage[0:4, 0:1],
                                                 id128[0:4, 0:4]),
                 reads=[res("vstage"), res("id128")], writes=[R_pb[5]])
            P.op("vector", lambda e, g=g: e.tensor_copy(nqrow[0:1, 4 * g:4 * g + 4],
                                                        pb[5][0:1, 0:4]),
                 reads=[R_pb[5]], writes=[res("nqrow")])
        for g in range(8):
            P.op("tensor", lambda e, g=g: e.transpose(pb[5][0:1, 0:4], nk48[0:4, g:g + 1],
                                                      id128[0:4, 0:4]),
                 reads=[res("nk48"), res("id128")], writes=[R_pb[5]])
            P.op("vector", lambda e, g=g: e.tensor_copy(nkrow[0:1, 4 * g:4 * g + 4],
                                                        pb[5][0:1, 0:4]),
                 reads=[R_pb[5]], writes=[res("nkrow")])
    P.barrier()

    o32h = sb_in(st_qo, "o32h", [128, 8, OWN], BF16)
    o32l = sb_in(st_qo, "o32l", [128, 8, OWN], BF16)

    with ExitStack() as ph:
        khead = [sb_in(ph, "kheadA", [128, TS], BF16), sb_in(ph, "kheadB", [128, TS], BF16)]
        qhead = [sb_in(ph, "qheadA", [128, OWN], BF16), sb_in(ph, "qheadB", [128, OWN], BF16)]
        vg = [sb_in(ph, "vgA", [128, 16, 132]), sb_in(ph, "vgB", [128, 16, 132])]
        ptile = [sb_in(ph, "ptA", [128, 512]), sb_in(ph, "ptB", [128, 512]),
                 sb_in(ph, "ptC", [128, 512])]
        tmp32 = sb_in(ph, "tmp32", [128, 512])
        zbcs = sb_in(ph, "zbcs", [32, 512])
        zr = sb_in(ph, "zr", [1, 1024])
        knm = [sb_in(ph, "knmA", [4, 4]), sb_in(ph, "knmB", [4, 4])]

        KSP = 128 * 8 * TS

        def fetch_head(h, buf):
            g, rrf = h // 4, h % 4
            dma("sync", khead[buf][0:32, :],
                bass.AP(kspill.tensor, 32 * rrf * 8 * TS + g * TS, [[8 * TS, 32], [1, TS]]),
                reads=[], writes=[res(f"khead{buf}")])
            dma("sync", khead[buf][32:64, :],
                bass.AP(kspill.tensor, KSP + 32 * rrf * 8 * TS + g * TS, [[8 * TS, 32], [1, TS]]),
                reads=[], writes=[res(f"khead{buf}")])
            dma("sync", khead[buf][64:96, :],
                bass.AP(kspill.tensor, 32 * rrf * 8 * TS + g * TS, [[8 * TS, 32], [1, TS]]),
                reads=[], writes=[res(f"khead{buf}")])
            qrd = [res(f"qbh_{g}_0"), res(f"qbh_{g}_1")]
            qrdl = [res(f"qbl_{g}_0"), res(f"qbl_{g}_1")]
            dma("sync", qhead[buf][0:32, :], qbh[32 * rrf:32 * rrf + 32, g, :],
                reads=qrd, writes=[res(f"qhead{buf}")])
            dma("sync", qhead[buf][32:64, :], qbh[32 * rrf:32 * rrf + 32, g, :],
                reads=qrd, writes=[res(f"qhead{buf}")])
            dma("sync", qhead[buf][64:96, :], qbl[32 * rrf:32 * rrf + 32, g, :],
                reads=qrdl, writes=[res(f"qhead{buf}")])

        def prep_group(g):
            gb_ = g % 2
            km = knm[gb_]
            kk = f"knm{gb_}"
            dma("sync", vg[gb_][:],
                bass.AP(vspill.tensor, g * 132, [[16 * NH * 33, 128], [NH * 33, 16], [1, 132]]),
                reads=[], writes=[res(f"vg{gb_}")])
            P.op("vector", lambda e, g=g, km=km: e.tensor_tensor(
                km[0:1, 0:4], nqrow[0:1, 4 * g:4 * g + 4], nkrow[0:1, 4 * g:4 * g + 4],
                op=ALU.mult),
                reads=[res("nqrow"), res("nkrow")], writes=[res(kk)])
            P.op("scalar", lambda e, km=km: e.activation(km[0:1, 0:4], km[0:1, 0:4], AF.Sqrt,
                                                  bias=0.0, scale=1.0),
                 reads=[res(kk)], writes=[res(kk)])
            P.op("vector", lambda e, km=km: e.tensor_scalar(km[0:1, 0:4], km[0:1, 0:4],
                                                     -SCALE, -1.0, op0=ALU.mult, op1=ALU.add),
                 reads=[], writes=[res(kk)])
            P.op("tensor", lambda e, km=km: e.matmul(pb[5][:, 0:4], ones1[:], km[0:1, 0:4],
                                              start=True, stop=True),
                 reads=[res("ones1"), res(kk)], writes=[R_pb[5]])
            P.op("vector", lambda e, g=g: e.tensor_copy(negm[:, 4 * g:4 * g + 4],
                                                        pb[5][:, 0:4]),
                 reads=[R_pb[5]], writes=[res(f"negm_{g}")])

        fetch_head(0, 0)
        prep_group(0)
        fin_prev = [None]
        for g in range(8):
            gb_ = g % 2

            for rr in range(4):
                h = 4 * g + rr
                hb = h % 2
                if h + 1 < 32:
                    fetch_head(h + 1, 1 - hb)
                iters = []
                for j in range(4):
                    iters.append(("diag", 0, j, None))
                for kc in (8, 9, 10, 11):
                    iters.append(("full", 0, kc, 0))
                for j in range(4):
                    iters.append(("diag", 1, j, None))
                for kc in (0, 1, 2, 3, 8, 9, 10, 11):
                    iters.append(("full", 1, kc, None))
                for kc in (12, 13, 14, 15):
                    iters.append(("full", 1, kc, 1))
                first_of_qc = {0: 0, 1: 8}
                last_of_qc = {0: 7, 1: 23}

                def emit_qk(it, it_i):
                    kind, qc, a, cnd = it
                    bs = it_i % 3
                    k0 = (qc * 512 + a * 128) if kind == "diag" else a * 128
                    P.op("tensor", lambda e, k0=k0, qc=qc, b=bs, hb=hb: e.matmul(
                        pb[b][:, 0:512], khead[hb][0:96, k0:k0 + 128],
                        qhead[hb][0:96, qc * 512:(qc + 1) * 512],
                        start=True, stop=True),
                        reads=[res(f"khead{hb}"), res(f"qhead{hb}")], writes=[R_pb[bs]])

                def emit_rest(it, it_i):
                    kind, qc, a, cnd = it
                    bs = it_i % 3
                    pv = it_i % 3
                    P.op("scalar", lambda e, b=bs, p=pv, h=h: e.activation(
                        ptile[p][:], pb[b][:, 0:512], AF.Exp,
                        bias=negm[:, h:h + 1], scale=SCALE),
                        reads=[R_pb[bs], res(f"negm_{g}")], writes=[res(f"pt{pv}")])
                    if kind == "diag":
                        P.op("vector", lambda e, p=pv, j=a: e.tensor_tensor(
                            ptile[p][:], ptile[p][:],
                            t01[:, 384 - 128 * j:896 - 128 * j], op=ALU.mult),
                            reads=[res("t01")], writes=[res(f"pt{pv}")])
                    elif cnd is not None:
                        P.op("vector", lambda e, p=pv, c=cnd: e.tensor_scalar(
                            ptile[p][:], ptile[p][:], gmbc[:, c:c + 1], None,
                            op0=ALU.mult),
                            reads=[res("gmbc")], writes=[res(f"pt{pv}")])
                    kc = (qc * 4 + a) if kind == "diag" else a
                    oa = pa[0:33, qc * 512:(qc + 1) * 512]
                    sa = (it_i == first_of_qc[qc])
                    so = (it_i == last_of_qc[qc])
                    off = 132 * kc + 33 * rr
                    P.op("tensor", lambda e, p=pv, oa=oa, sa=sa, so=so, off=off, gb_=gb_: e.matmul(
                        oa, vg[gb_][:].rearrange("p a b -> p (a b)")[:, off:off + 33],
                        ptile[p][:], start=sa, stop=so),
                        reads=[res(f"vg{gb_}"), res(f"pt{pv}")], writes=[R_pa[qc]])

                emit_qk(iters[0], 0)
                emit_qk(iters[1], 1)
                if fin_prev[0] is not None:
                    fin_prev[0]()
                    fin_prev[0] = None
                if rr == 1 and g + 1 < 8:
                    prep_group(g + 1)
                for it_i in range(len(iters)):
                    if it_i + 2 < len(iters):
                        emit_qk(iters[it_i + 2], it_i + 2)
                    emit_rest(iters[it_i], it_i)
                def fin(g=g, rr=rr):
                    P.op("vector", lambda e: e.reciprocal(zr[0:1, :], pa[32:33, 0:1024]),
                         reads=[R_pa[0], R_pa[1]], writes=[res("zr")])
                    for qc2 in range(2):
                        zb = 3 + qc2
                        P.op("tensor", lambda e, qc2=qc2, zb=zb: e.matmul(
                            pb[zb][0:32, 0:512], ones1[0:1, 0:32],
                            zr[0:1, qc2 * 512:(qc2 + 1) * 512], start=True, stop=True),
                            reads=[res("ones1"), res("zr")], writes=[R_pb[zb]])
                        P.op("scalar", lambda e, zb=zb: e.activation(
                            zbcs[:], pb[zb][0:32, 0:512], AF.Copy, bias=0.0, scale=1.0),
                            reads=[R_pb[zb]], writes=[res("zbcs")])
                        P.op("vector", lambda e, qc2=qc2, rr=rr: e.tensor_tensor(
                            tmp32[32 * rr:32 * rr + 32, :],
                            pa[0:32, qc2 * 512:(qc2 + 1) * 512],
                            zbcs[:], op=ALU.mult),
                            reads=[R_pa[qc2], res("zbcs")], writes=[res("tmp32")])
                        hk = [res(f"o32h_{g}_{rr}_{q8_}")
                              for q8_ in range(qc2 * 4, qc2 * 4 + 4)]
                        lk = [res(f"o32l_{g}_{rr}_{q8_}")
                              for q8_ in range(qc2 * 4, qc2 * 4 + 4)]
                        P.op("vector", lambda e, qc2=qc2, rr=rr, g=g: e.tensor_copy(
                            o32h[32 * rr:32 * rr + 32, g, qc2 * 512:(qc2 + 1) * 512],
                            tmp32[32 * rr:32 * rr + 32, :]),
                            reads=[res("tmp32")], writes=hk)
                        P.op("vector", lambda e, qc2=qc2, rr=rr, g=g: e.scalar_tensor_tensor(
                            o32l[32 * rr:32 * rr + 32, g, qc2 * 512:(qc2 + 1) * 512],
                            tmp32[32 * rr:32 * rr + 32, :], 1.0,
                            o32h[32 * rr:32 * rr + 32, g, qc2 * 512:(qc2 + 1) * 512],
                            op0=ALU.mult, op1=ALU.subtract),
                            reads=[res("tmp32")] + hk, writes=lk)
                fin_prev[0] = fin
        fin_prev[0]()
    P.barrier()

    with ExitStack() as ph:
        b2st = sb_in(ph, "b2st", [1, 512])
        rhh = [sb_in(ph, "rhhA", [128, 512], BF16), sb_in(ph, "rhhB", [128, 512], BF16)]
        rhl = [sb_in(ph, "rhlA", [128, 512], BF16), sb_in(ph, "rhlB", [128, 512], BF16)]
        y1sw = sb_in(ph, "y1sw", [128, 4, D])
        xst = [sb_in(ph, "xstA", [128, 512]), sb_in(ph, "xstB", [128, 512])]
        st3 = sb_in(ph, "st3", [128, 2])


        def fetch_rh(ot, cc):
            rb = cc % 2
            for h2_, dst, rn in ((0, rhh, "rhh"), (1, rhl, "rhl")):
                dma("sync", dst[rb][:],
                    bass.AP(wt2_scr.tensor, h2_ * 128 * 8 * D + cc * D + ot * 512,
                            [[8 * D, 128], [1, 512]]),
                    reads=[], writes=[res(f"{rn}{rb}")])

        fetch_rh(0, 0)
        for sweep in range(2):
            for ot in range(8):
                dma("sync", b2st[:], b2[None, ot * 512:(ot + 1) * 512],
                    reads=[], writes=[res("b2st")])
                for cc in range(8):
                    rb = cc % 2
                    if cc + 1 < 8:
                        fetch_rh(ot, cc + 1)
                    elif ot + 1 < 8:
                        fetch_rh(ot + 1, 0)
                    elif sweep == 0:
                        fetch_rh(0, 0)
                    for qq in range(4):
                        qb = sweep * 4 + qq
                        trip = ((o32h, rhh, "o32h", f"rhh{rb}", cc == 0),
                                (o32h, rhl, "o32h", f"rhl{rb}", False),
                                (o32l, rhh, "o32l", f"rhh{rb}", False))
                        for (lo_, rx, lrn, rrn, sa_) in trip:
                            P.op("tensor", lambda e, cc=cc, qb=qb, qq=qq, rb=rb,
                                 lo_=lo_, rx=rx, sa_=sa_: e.matmul(
                                pb[qq][:, 0:512], lo_[:, cc, qb * 128:(qb + 1) * 128],
                                rx[rb][:], start=sa_, stop=False),
                                reads=[res(f"{lrn}_{cc}_{rr_}_{qb}") for rr_ in range(4)]
                                      + [res(rrn)],
                                writes=[R_pb[qq]])
                for qq in range(4):
                    qb = sweep * 4 + qq
                    xb = qq % 2
                    P.op("tensor", lambda e, qq=qq: e.matmul(
                        pb[qq][:, 0:512], ones1[:], b2st[:],
                        start=False, stop=True),
                        reads=[res("ones1"), res("b2st")], writes=[R_pb[qq]])
                    dma("sync", xst[xb][:], x[qb * 128:(qb + 1) * 128, ot * 512:(ot + 1) * 512],
                        reads=[], writes=[res(f"xst{xb}")])
                    P.op("vector", lambda e, qq=qq, ot=ot, xb=xb: e.tensor_tensor(
                        y1sw[:, qq, ot * 512:(ot + 1) * 512], pb[qq][:, 0:512],
                        xst[xb][:], op=ALU.add),
                        reads=[R_pb[qq], res(f"xst{xb}")], writes=[res(f"y1sw_{qq}")])
            for qq in range(4):
                qb = sweep * 4 + qq
                P.op("vector", lambda e, qq=qq: e.reduce_sum(st3[:, 0:1],
                                                             y1sw[:, qq, :], axis=X),
                     reads=[res(f"y1sw_{qq}")], writes=[res("st3")])
                P.op("vector", lambda e, qb=qb: e.tensor_scalar(
                    ln2m[:, qb:qb + 1], st3[:, 0:1], 1.0 / D, None, op0=ALU.mult),
                    reads=[res("st3")], writes=[res("ln2m")])
                dma("gpsimd", y1scr[qb], y1sw[:, qq, :],
                    reads=[res(f"y1sw_{qq}")], writes=[res(f"y1scr_{qb}")])
    P.barrier()
    st_qo.close()

    st_h = ExitStack()
    h2h = sb_in(st_h, "h2h", [128, 32, OWN], BF16)
    h2l = sb_in(st_h, "h2l", [128, 32, OWN], BF16)
    umT = sb_in(st_h, "umT", [64, OWN])
    btsb = sb_in(st_h, "btsb", [64, D])
    with ExitStack() as ph:
        agh = sb_in(ph, "agh", [128, 32, 68], BF16)
        agl = sb_in(ph, "agl", [128, 32, 68], BF16)
        with ExitStack() as pp:
            lasb = sb_in(pp, "lasb", [64, D])
            gwsb = sb_in(pp, "gwsb", [4, D])
            bsl = sb_in(pp, "bsl", [128, 64])
            dma("sync", lasb[:], la, writes=[res("lasb")])
            dma("sync", gwsb[:], gw, writes=[res("gwsb")])
            for dc in range(32):
                b6 = 2 + (dc % 2)
                P.op("tensor", lambda e, dc=dc, b=b6: e.transpose(
                    pb[b][:, 0:64], lasb[0:64, dc * 128:(dc + 1) * 128],
                    id128[0:64, 0:64]),
                    reads=[res("lasb"), res("id128")], writes=[R_pb[b6]])
                P.op("vector", lambda e, dc=dc, b=b6: e.tensor_copy(
                    agh[:, dc, 0:64], pb[b][:, 0:64]),
                    reads=[R_pb[b6]], writes=[res(f"agh_{dc}")])
                P.op("vector", lambda e, dc=dc, b=b6: e.scalar_tensor_tensor(
                    agl[:, dc, 0:64], pb[b][:, 0:64], 1.0, agh[:, dc, 0:64],
                    op0=ALU.mult, op1=ALU.subtract),
                    reads=[R_pb[b6], res(f"agh_{dc}")], writes=[res(f"agl_{dc}")])
                P.op("tensor", lambda e, dc=dc, b=b6: e.transpose(
                    pb[b][:, 64:68], gwsb[0:4, dc * 128:(dc + 1) * 128],
                    id128[0:4, 0:4]),
                    reads=[res("gwsb"), res("id128")], writes=[R_pb[b6]])
                P.op("vector", lambda e, dc=dc, b=b6: e.tensor_copy(
                    agh[:, dc, 64:68], pb[b][:, 64:68]),
                    reads=[R_pb[b6]], writes=[res(f"agh_{dc}")])
                P.op("vector", lambda e, dc=dc, b=b6: e.scalar_tensor_tensor(
                    agl[:, dc, 64:68], pb[b][:, 64:68], 1.0, agh[:, dc, 64:68],
                    op0=ALU.mult, op1=ALU.subtract),
                    reads=[R_pb[b6], res(f"agh_{dc}")], writes=[res(f"agl_{dc}")])
            for sl in range(32):
                dma("sync", bsl[:], lb[sl * 128:(sl + 1) * 128, :],
                    reads=[], writes=[res("bsl")])
                b6 = 2 + (sl % 2)
                P.op("tensor", lambda e, b=b6: e.transpose(pb[b][0:64, 0:128],
                                                           bsl[:], id128[:]),
                     reads=[res("bsl"), res("id128")], writes=[R_pb[b6]])
                P.op("vector", lambda e, sl=sl, b=b6: e.tensor_copy(
                    btsb[0:64, sl * 128:(sl + 1) * 128], pb[b][0:64, 0:128]),
                    reads=[R_pb[b6]], writes=[res(f"btsb_{sl}")])

        y1sl = [sb_in(ph, "y1slA", [128, D]), sb_in(ph, "y1slB", [128, D])]
        st4 = [sb_in(ph, "st4A", [128, 6]), sb_in(ph, "st4B", [128, 6])]
        usb = sb_in(ph, "usb", [128, 68])
        msk = sb_in(ph, "msk", [128, 64])
        sg = sb_in(ph, "sg", [128, 8])
        for qb in range(8):
            yb = qb % 2
            dma("sync", y1sl[yb][:], y1scr[qb], reads=[], writes=[res(f"y1sl{yb}")])
            dum = h2h[:, :, qb * 128:(qb + 1) * 128]
            s4 = st4[yb]
            s4k = f"st4{yb}"
            P.op("scalar", lambda e, d=dum, yb=yb, s4=s4: e.activation(d, y1sl[yb][:], AF.Square,
                                                                bias=0.0, scale=1.0,
                                                                accum_out=s4[:, 1:2]),
                 reads=[res(f"y1sl{yb}")],
                 writes=[res(s4k)] + [res(f"h2h_{dc}_{qb}") for dc in range(32)])
            P.op("vector", lambda e, qb=qb, s4=s4: e.tensor_tensor(
                s4[:, 5:6], ln2m[:, qb:qb + 1], ln2m[:, qb:qb + 1], op=ALU.mult),
                reads=[res("ln2m")], writes=[res(s4k)])
            P.op("vector", lambda e, s4=s4: e.scalar_tensor_tensor(
                s4[:, 2:3], s4[:, 1:2], 1.0 / D, s4[:, 5:6],
                op0=ALU.mult, op1=ALU.subtract),
                reads=[], writes=[res(s4k)])
            P.op("scalar", lambda e, s4=s4: e.activation(s4[:, 3:4], s4[:, 2:3], AF.Sqrt,
                                                  bias=float(EPS), scale=1.0),
                 reads=[res(s4k)], writes=[res(s4k)])
            P.op("vector", lambda e, s4=s4: e.reciprocal(s4[:, 3:4], s4[:, 3:4]),
                 reads=[], writes=[res(s4k)])
            P.op("vector", lambda e, qb=qb, s4=s4: e.tensor_tensor(
                s4[:, 4:5], ln2m[:, qb:qb + 1], s4[:, 3:4], op=ALU.mult),
                reads=[res("ln2m")], writes=[res(s4k)])
            P.op("vector", lambda e, s4=s4: e.tensor_scalar(s4[:, 4:5], s4[:, 4:5], -1.0,
                                                     None, op0=ALU.mult),
                 reads=[], writes=[res(s4k)])
            P.op("scalar", lambda e, yb=yb, s4=s4: e.activation(y1sl[yb][:], y1sl[yb][:],
                                                  AF.Identity,
                                                  bias=s4[:, 4:5], scale=s4[:, 3:4]),
                 reads=[res(s4k)], writes=[res(f"y1sl{yb}")])
            for dc in range(32):
                b6 = dc % 4
                P.op("tensor", lambda e, dc=dc, b=b6, yb=yb: e.transpose(
                    pb[b][:, 0:128], y1sl[yb][:, dc * 128:(dc + 1) * 128], id128[:]),
                    reads=[res(f"y1sl{yb}"), res("id128")], writes=[R_pb[b6]])
                P.op("scalar", lambda e, dc=dc, qb=qb, b=b6: e.activation(
                    h2h[:, dc, qb * 128:(qb + 1) * 128], pb[b][:, 0:128],
                    AF.Copy, bias=0.0, scale=1.0),
                    reads=[R_pb[b6]], writes=[res(f"h2h_{dc}_{qb}")])
                P.op("vector", lambda e, dc=dc, qb=qb, b=b6: e.scalar_tensor_tensor(
                    h2l[:, dc, qb * 128:(qb + 1) * 128], pb[b][:, 0:128], 1.0,
                    h2h[:, dc, qb * 128:(qb + 1) * 128],
                    op0=ALU.mult, op1=ALU.subtract),
                    reads=[R_pb[b6], res(f"h2h_{dc}_{qb}")], writes=[res(f"h2l_{dc}_{qb}")])
            for dc in range(32):
                trip = ((h2h, agh, "h2h", "agh", dc == 0, False),
                        (h2h, agl, "h2h", "agl", False, False),
                        (h2l, agh, "h2l", "agl", False, dc == 31))
                for (lx, rx, lrn, rrn, sa_, so_) in trip:
                    P.op("tensor", lambda e, dc=dc, qb=qb, lx=lx, rx=rx, sa_=sa_, so_=so_: e.matmul(
                        pb[4][:, 0:68], lx[:, dc, qb * 128:(qb + 1) * 128],
                        rx[:, dc, :], start=sa_, stop=so_),
                        reads=[res(f"{lrn}_{dc}_{qb}"), res(f"agh_{dc}"), res(f"agl_{dc}")],
                        writes=[R_pb[4]])
            P.op("vector", lambda e: e.tensor_copy(usb[:, 0:64], pb[4][:, 0:64]),
                 reads=[R_pb[4]], writes=[res("usb")])
            P.op("vector", lambda e: e.tensor_tensor(usb[:, 64:68], pb[4][:, 64:68],
                                                     gbbc[:], op=ALU.add),
                 reads=[R_pb[4], res("gbbc")], writes=[res("usb")])
            s_ = [usb[:, 64 + i:65 + i] for i in range(4)]
            P.op("vector", lambda e: e.tensor_tensor(sg[:, 0:1], s_[2], s_[3], op=ALU.max),
                 reads=[res("usb")], writes=[res("sg")])
            P.op("vector", lambda e: e.tensor_tensor(sg[:, 1:2], sg[:, 0:1], s_[1], op=ALU.max),
                 reads=[res("usb")], writes=[res("sg")])
            P.op("vector", lambda e: e.tensor_tensor(sg[:, 2:3], sg[:, 1:2], s_[0], op=ALU.is_gt),
                 reads=[res("usb")], writes=[res("sg")])
            P.op("vector", lambda e: e.tensor_tensor(sg[:, 3:4], s_[0], s_[1], op=ALU.max),
                 reads=[res("usb")], writes=[res("sg")])
            P.op("vector", lambda e: e.tensor_tensor(sg[:, 4:5], sg[:, 0:1], sg[:, 3:4], op=ALU.is_gt),
                 reads=[], writes=[res("sg")])
            P.op("vector", lambda e: e.tensor_tensor(sg[:, 5:6], sg[:, 3:4], s_[2], op=ALU.max),
                 reads=[res("usb")], writes=[res("sg")])
            P.op("vector", lambda e: e.tensor_tensor(sg[:, 6:7], s_[3], sg[:, 5:6], op=ALU.is_gt),
                 reads=[res("usb")], writes=[res("sg")])
            P.op("vector", lambda e: e.tensor_scalar(sg[:, 7:8], sg[:, 2:3], 8.0, 8.0,
                                                     op0=ALU.mult, op1=ALU.add),
                 reads=[], writes=[res("sg")])
            P.op("vector", lambda e: e.scalar_tensor_tensor(sg[:, 7:8], sg[:, 4:5], 16.0,
                                                            sg[:, 7:8], op0=ALU.mult, op1=ALU.add),
                 reads=[], writes=[res("sg")])
            P.op("vector", lambda e: e.scalar_tensor_tensor(sg[:, 7:8], sg[:, 6:7], 32.0,
                                                            sg[:, 7:8], op0=ALU.mult, op1=ALU.add),
                 reads=[], writes=[res("sg")])
            P.op("vector", lambda e: e.tensor_scalar(msk[:], iota64[:], sg[:, 7:8], None,
                                                     op0=ALU.is_lt),
                 reads=[res("iota64"), res("sg")], writes=[res("msk")])
            P.op("vector", lambda e: e.tensor_tensor(msk[:], msk[:], usb[:, 0:64], op=ALU.mult),
                 reads=[res("usb")], writes=[res("msk")])
            P.op("tensor", lambda e: e.transpose(pb[2][0:64, 0:128], msk[:], id128[:]),
                 reads=[res("msk"), res("id128")], writes=[R_pb[2]])
            P.op("vector", lambda e, qb=qb: e.tensor_copy(
                umT[0:64, qb * 128:(qb + 1) * 128], pb[2][0:64, 0:128]),
                reads=[R_pb[2]], writes=[res(f"umT_{qb}")])
    P.barrier()

    with ExitStack() as ph:
        wT5h = [sb_in(ph, "wT5hA", [128, D], BF16), sb_in(ph, "wT5hB", [128, D], BF16)]
        wT5l = [sb_in(ph, "wT5lA", [128, D], BF16), sb_in(ph, "wT5lB", [128, D], BF16)]
        y2fm = [sb_in(ph, "y2fmA", [128, 1024]), sb_in(ph, "y2fmB", [128, 1024])]
        y2tm = [sb_in(ph, "y2tmA", [128, 128]), sb_in(ph, "y2tmB", [128, 128]),
                sb_in(ph, "y2tmC", [128, 128]), sb_in(ph, "y2tmD", [128, 128])]
        y1t = [[sb_in(ph, f"y1t{p}_{i}", [128, 128]) for i in range(8)] for p in range(2)]

        def fetch_y1(oc):
            par = oc % 2
            for qb in range(8):
                dma("sync", y1t[par][qb][:],
                    bass.AP(y1scr.tensor, qb * 128 * D + oc * 128, [[D, 128], [1, 128]]),
                    reads=[], writes=[res(f"y1t_{par}_{qb}")])

        def fetch5(oc):
            b = oc % 2
            for h2_, dst, rn in ((0, wT5h, "wT5h"), (1, wT5l, "wT5l")):
                dma("sync", dst[b][:],
                    bass.AP(wtf_scr.tensor, h2_ * 128 * 32 * D + oc * D,
                            [[32 * D, 128], [1, D]]),
                    reads=[], writes=[res(f"{rn}{b}")])

        def mm5(oc, dc):
            b = oc % 2
            c0 = dc * 128
            for qt in range(2):
                trip = ((wT5h, h2h, f"wT5h{b}", "h2h", dc == 0),
                        (wT5h, h2l, f"wT5h{b}", "h2l", False),
                        (wT5l, h2h, f"wT5l{b}", "h2h", False))
                for (lw, rx, lrn, rrn, sa_) in trip:
                    P.op("tensor", lambda e, dc=dc, qt=qt, b=b, c0=c0, lw=lw, rx=rx,
                         sa_=sa_: e.matmul(
                        pb[qt][:, 0:512], lw[b][:, c0:c0 + 128],
                        rx[:, dc, qt * 512:(qt + 1) * 512],
                        start=sa_, stop=False),
                        reads=[res(lrn)] + [res(f"{rrn}_{dc}_{qb_}")
                                            for qb_ in range(4 * qt, 4 * qt + 4)],
                        writes=[R_pb[qt]])

        def finish5(oc):
            po = oc % 2
            for qt in range(2):
                P.op("tensor", lambda e, oc=oc, qt=qt: e.matmul(
                    pb[qt][:, 0:512], btsb[0:64, oc * 128:(oc + 1) * 128],
                    umT[0:64, qt * 512:(qt + 1) * 512], start=False, stop=True),
                    reads=[res(f"btsb_{oc}")] + [res(f"umT_{qb_}")
                                                 for qb_ in range(4 * qt, 4 * qt + 4)],
                    writes=[R_pb[qt]])
                P.op("vector", lambda e, oc=oc, qt=qt, po=po: e.tensor_scalar(
                    y2fm[po][:, qt * 512:(qt + 1) * 512], pb[qt][:, 0:512],
                    bsb[:, oc:oc + 1], None, op0=ALU.add),
                    reads=[R_pb[qt], res("bsb")], writes=[res(f"y2fm{po}")])
            if oc + 1 < 32:
                fetch_y1(oc + 1)
            par = oc % 2
            for qb in range(8):
                b6 = 2 + (qb % 4)
                pt2 = qb % 4
                P.op("tensor", lambda e, qb=qb, b=b6, po=po: e.transpose(
                    pb[b][:, 0:128], y2fm[po][:, qb * 128:(qb + 1) * 128], id128[:]),
                    reads=[res(f"y2fm{po}"), res("id128")], writes=[R_pb[b6]])
                P.op("vector", lambda e, b=b6, pt2=pt2, par=par, qb=qb: e.tensor_tensor(
                    y2tm[pt2][:], pb[b][:, 0:128], y1t[par][qb][:], op=ALU.add),
                     reads=[R_pb[b6], res(f"y1t_{par}_{qb}")], writes=[res(f"y2tm{pt2}")])
                dma("gpsimd",
                    bass.AP(y.tensor, qb * 128 * D + oc * 128, [[D, 128], [1, 128]]),
                    y2tm[pt2][:], reads=[res(f"y2tm{pt2}")], writes=[res(f"yout_{oc}_{qb}")])

        fetch5(0)
        fetch5(1)
        fetch_y1(0)
        for oc in range(32):
            for dc in range(32):
                mm5(oc, dc)
            if oc + 2 < 32:
                fetch5(oc + 2)
            finish5(oc)
    st_h.close()

    final_lanes = [(l, P.cnt[l]) for l in LANES if l.startswith("gpsimd") and P.cnt[l] > 0]
    with nc.Block() as block:
        def mk(engname):
            def body(eng):
                hi_waited = {}
                for fn, waits, key in P.ops[engname]:
                    for (o, c) in waits:
                        if c > hi_waited.get(o, 0):
                            eng.wait_ge(sems[o], c)
                            hi_waited[o] = c
                    if fn == "waits_only":
                        pass
                    elif key is None:
                        fn(eng)
                    else:
                        fn(eng).then_inc(sems[key], 16 if key not in ENGS else 1)
                if engname == "gpsimd":
                    for (l, c) in final_lanes:
                        eng.wait_ge(sems[l], c)
            return body
        block.sync(mk("sync"))
        block.scalar(mk("scalar"))
        block.vector(mk("vector"))
        block.tensor(mk("tensor"))
        block.gpsimd(mk("gpsimd"))

    top.close()
    return nc


_PERM_CHUNKS = {0: [0, 3, 1, 2], 1: [1, 2, 0, 3]}


def _make_sched(parity):
    return np.array([[1.0 if parity == 1 else 0.0,
                      1.0 if parity == 0 else 0.0]], dtype=np.float32)


def kernel(L0_attn_x, L0_attn_W1, L0_attn_b1, L0_attn_W2, L0_attn_b2,
           L0_ffn_W, L0_ffn_b, L0_ffn_A, L0_ffn_B, L0_ffn_gate_w, L0_ffn_gate_b):
    xf = np.ascontiguousarray(np.asarray(L0_attn_x, dtype=np.float32))
    w1 = np.ascontiguousarray(np.asarray(L0_attn_W1, dtype=np.float32))
    b1v = np.ascontiguousarray(np.asarray(L0_attn_b1, dtype=np.float32))
    w2 = np.ascontiguousarray(np.asarray(L0_attn_W2, dtype=np.float32))
    b2v = np.ascontiguousarray(np.asarray(L0_attn_b2, dtype=np.float32))
    wfv = np.ascontiguousarray(np.asarray(L0_ffn_W, dtype=np.float32))
    bfv = np.ascontiguousarray(np.asarray(L0_ffn_b, dtype=np.float32))
    lav = np.ascontiguousarray(np.asarray(L0_ffn_A, dtype=np.float32))
    lbv = np.ascontiguousarray(np.asarray(L0_ffn_B, dtype=np.float32))
    gwv = np.ascontiguousarray(np.asarray(L0_ffn_gate_w, dtype=np.float32))
    gbv = np.ascontiguousarray(np.asarray(L0_ffn_gate_b, dtype=np.float32))

    nc = build_program()
    in_maps = []
    for c in range(8):
        s, p = c // 2, c % 2
        perm = np.concatenate([np.arange(g * 512, (g + 1) * 512) for g in _PERM_CHUNKS[p]])
        in_maps.append({
            "x": np.ascontiguousarray(xf[s][perm]),
            "w1": w1, "b1": b1v, "w2": w2, "b2": b2v,
            "wf": wfv, "bf": bfv, "la": lav, "lb": lbv, "gw": gwv, "gb": gbv,
            "sched": _make_sched(p),
        })
    resu = run_bass_kernel_spmd(nc, in_maps, core_ids=list(range(8)))
    out = np.empty((4, TS, D), dtype=np.float32)
    for c in range(8):
        s, p = c // 2, c % 2
        shard = np.asarray(resu.results[c]["y"], dtype=np.float32)
        for i, g in enumerate(_PERM_CHUNKS[p][:2]):
            out[s, g * 512:(g + 1) * 512, :] = shard[i * 512:(i + 1) * 512, :]
    return out


# version 22 (final)
# speedup vs baseline: 1.0675x; 1.0675x over previous
import math
from contextlib import ExitStack

import numpy as np
import concourse.bass as bass
import concourse.mybir as mybir
from concourse.bass_utils import run_bass_kernel_spmd
from concourse.expressions_rust import RuntimeValue

F32 = mybir.dt.float32
BF16 = mybir.dt.bfloat16
I32 = mybir.dt.int32
AF = mybir.ActivationFunctionType
ALU = mybir.AluOpType
ds = bass.ds
X = mybir.AxisListType.X

D = 4096
TS = 2048
OWN = 1024
NH = 32
SCALE = 1.0 / math.sqrt(128.0)
EPS = 1e-5

ENGS = ("sync", "scalar", "vector", "tensor", "gpsimd")


class Res:
    __slots__ = ("w", "rs")

    def __init__(self):
        self.w = None
        self.rs = []


NLANE = 8
LANES = [f"{e}D{i}" for e in ("sync", "gpsimd", "scalar") for i in range(NLANE)]


class Plan:

    def __init__(self):
        self.ops = {e: [] for e in ENGS}
        self.cnt = {e: 0 for e in ENGS}
        for l in LANES:
            self.cnt[l] = 0
        self.dma_i = {e: 0 for e in ENGS}

    def op(self, eng, fn, reads=(), writes=(), dma=False, extra_waits=()):
        waits = {}

        def add(ec):
            if ec is None:
                return
            e, c = ec
            if e == eng == "tensor":
                return
            if waits.get(e, 0) < c:
                waits[e] = c
        for r in reads:
            add(r.w)
        for r in writes:
            add(r.w)
            for rd in r.rs:
                add(rd)
        for ec in extra_waits:
            add(ec)
        if dma:
            lane = f"{eng}D{self.dma_i[eng] % NLANE}"
            self.dma_i[eng] += 1
            if self.cnt[lane] > waits.get(lane, 0):
                waits[lane] = self.cnt[lane]
            self.cnt[lane] += 16
            me = (lane, self.cnt[lane])
            key = lane
        else:
            self.cnt[eng] += 1
            me = (eng, self.cnt[eng])
            key = eng
        for r in reads:
            r.rs.append(me)
        for r in writes:
            r.w = me
            r.rs = []
        self.ops[eng].append((fn, sorted(waits.items()), key))

    def raw(self, eng, fn):
        self.ops[eng].append((fn, [], None))

    def barrier(self):
        snap = {k: v for k, v in self.cnt.items() if v > 0}
        for e in ENGS:
            waits = [(o, c) for o, c in snap.items() if o != e]
            self.ops[e].append(("waits_only", waits, None))


def build_program():
    nc = bass.Bass("TRN2", target_bir_lowering=False)

    x = nc.dram_tensor("x", [TS, D], F32, kind="ExternalInput").ap()
    w1 = nc.dram_tensor("w1", [6144, D], F32, kind="ExternalInput").ap()
    b1 = nc.dram_tensor("b1", [6144], F32, kind="ExternalInput").ap()
    w2 = nc.dram_tensor("w2", [D, D], F32, kind="ExternalInput").ap()
    b2 = nc.dram_tensor("b2", [D], F32, kind="ExternalInput").ap()
    wf = nc.dram_tensor("wf", [D, D], F32, kind="ExternalInput").ap()
    bf = nc.dram_tensor("bf", [D], F32, kind="ExternalInput").ap()
    la = nc.dram_tensor("la", [64, D], F32, kind="ExternalInput").ap()
    lb = nc.dram_tensor("lb", [D, 64], F32, kind="ExternalInput").ap()
    gw = nc.dram_tensor("gw", [4, D], F32, kind="ExternalInput").ap()
    gb = nc.dram_tensor("gb", [4], F32, kind="ExternalInput").ap()
    sched = nc.dram_tensor("sched", [1, 2], F32, kind="ExternalInput").ap()
    y = nc.dram_tensor("y", [OWN, D], F32, kind="ExternalOutput").ap()

    kspill = nc.dram_tensor("kspill", [2, 128, 8, TS], BF16, kind="Internal").ap()
    vspill = nc.dram_tensor("vspill", [128, 16, NH * 33], F32, kind="Internal").ap()
    y1scr = nc.dram_tensor("y1scr", [8, 128, D], F32, kind="Internal").ap()
    wt1_scr = nc.dram_tensor("wt1_scr", [2, 128, 24, D], BF16, kind="Internal").ap()
    wtf_scr = nc.dram_tensor("wtf_scr", [2, 128, 32, D], BF16, kind="Internal").ap()
    wt2_scr = nc.dram_tensor("wt2_scr", [2, 128, 8, D], BF16, kind="Internal").ap()


    t01_np = np.zeros((128, 896), dtype=np.float32)
    for pp in range(128):
        t01_np[pp, 384 + pp:] = 1.0
    t01_h = nc.inline_tensor(t01_np, name="t01c")
    id_h = nc.inline_tensor(np.eye(128, dtype=np.float32), name="idc")
    iota_h = nc.inline_tensor(
        np.ascontiguousarray(np.broadcast_to(np.arange(64, dtype=np.float32), (128, 64))),
        name="iotac")
    r128_np = np.zeros((128, 32), dtype=np.float32)
    for pp in range(128):
        r128_np[pp, pp // 4] = 1.0
    r128_h = nc.inline_tensor(r128_np, name="r128c")
    h4_np = np.zeros((128, 4), dtype=np.float32)
    for pp in range(128):
        h4_np[pp, pp // 32] = 1.0
    h4_h = nc.inline_tensor(h4_np, name="h4c")

    P = Plan()
    top = ExitStack()
    sems = {e: top.enter_context(nc.semaphore(f"sm_{e}")) for e in ENGS}
    for l in LANES:
        sems[l] = top.enter_context(nc.semaphore(f"sm_{l}"))

    def sb_in(stack, name, shape, dtype=F32):
        return stack.enter_context(nc.sbuf_tensor(name, shape, dtype))

    t01 = sb_in(top, "t01", [128, 896])
    id128 = sb_in(top, "id128", [128, 128])
    iota64 = sb_in(top, "iota64", [128, 64])
    r128 = sb_in(top, "r128", [128, 32])
    h4 = sb_in(top, "h4", [128, 4])
    ones1 = sb_in(top, "ones1", [1, 128])
    zrow = sb_in(top, "zrow", [1, 512])
    ssb = sb_in(top, "ssb", [1, 2])
    gmbc = sb_in(top, "gmbc", [128, 2])
    b1f = sb_in(top, "b1f", [128, 24])
    b1q4 = sb_in(top, "b1q4", [128, 8, 4])
    bsb = sb_in(top, "bsb", [128, 32])
    gbsb = sb_in(top, "gbsb", [1, 4])
    gbbc = sb_in(top, "gbbc", [128, 4])
    negm = sb_in(top, "negm", [128, 32])
    nqrow = sb_in(top, "nqrow", [1, 32])
    nkrow = sb_in(top, "nkrow", [1, 32])
    nk48 = sb_in(top, "nk48", [4, 8])
    nk4t = sb_in(top, "nk4t", [4, 1])
    ln2m = sb_in(top, "ln2m", [128, 8])
    epsc = sb_in(top, "epsc", [128, 1])
    sclc = sb_in(top, "sclc", [128, 1])

    pb = [top.enter_context(nc.psum_tensor(f"pb{i}", [128, 512], F32)) for i in range(6)]
    pa = top.enter_context(nc.psum_tensor("pa", [128, 1024], F32))
    R_pb = [Res() for _ in range(6)]
    R_pa = [Res(), Res()]

    rs = {}

    def res(name):
        if name not in rs:
            rs[name] = Res()
        return rs[name]

    def dma(eng, out_ap, in_ap, reads=(), writes=(), slow=False):
        if slow:
            P.op(eng, lambda e, o=out_ap, i=in_ap: e.dma_start(
                out=o, in_=i, allow_slow_non_contiguous=True),
                reads=reads, writes=writes, dma=True)
        else:
            P.op(eng, lambda e, o=out_ap, i=in_ap: e.dma_start(out=o, in_=i),
                 reads=reads, writes=writes, dma=True)

    dma("sync", t01[:], t01_h.ap(), writes=[res("t01")])
    dma("sync", id128[:], id_h.ap(), writes=[res("id128")])
    dma("sync", iota64[:], iota_h.ap(), writes=[res("iota64")])
    dma("sync", r128[:], r128_h.ap(), writes=[res("r128")])
    dma("sync", h4[:], h4_h.ap(), writes=[res("h4")])
    dma("sync", ssb[:], sched, writes=[res("ssb")])
    P.op("vector", lambda e: e.memset(ones1[:], 1.0), writes=[res("ones1")])
    P.op("vector", lambda e: e.memset(epsc[:], float(EPS)), writes=[res("epsc")])
    P.op("vector", lambda e: e.memset(sclc[:], float(SCALE)), writes=[res("sclc")])
    nc.const_aps.register(F32, float(EPS), epsc[:])
    nc.const_aps.register(F32, float(SCALE), sclc[:])
    P.op("vector", lambda e: e.memset(zrow[:], 0.0), writes=[res("zrow")])
    dma("sync", b1q4[:], bass.AP(b1.tensor, 0, [[4, 128], [512, 8], [1, 4]]),
        writes=[res("b1q4")])
    P.op("vector", lambda e: e.reduce_sum(b1f[:, 0:8].unsqueeze(2), b1q4[:], axis=X),
         reads=[res("b1q4")], writes=[res("b1f")])
    dma("sync", b1f[:, 8:24], bass.AP(b1.tensor, 4096, [[1, 128], [128, 16]]),
        writes=[res("b1f")], slow=True)
    dma("sync", bsb[:], bass.AP(bf.tensor, 0, [[1, 128], [128, 32]]),
        writes=[res("bsb")], slow=True)
    dma("sync", gbsb[:], gb[None, :], writes=[res("gbsb")])
    P.op("tensor", lambda e: e.matmul(pb[0][:, 0:4], ones1[:], gbsb[:], start=True, stop=True),
         reads=[res("ones1"), res("gbsb")], writes=[R_pb[0]])
    P.op("vector", lambda e: e.tensor_copy(gbbc[:], pb[0][:, 0:4]),
         reads=[R_pb[0]], writes=[res("gbbc")])
    P.op("tensor", lambda e: e.matmul(pb[0][:, 0:2], ones1[:], ssb[:], start=True, stop=True),
         reads=[res("ones1"), res("ssb")], writes=[R_pb[0]])
    P.op("vector", lambda e: e.tensor_copy(gmbc[:], pb[0][:, 0:2]),
         reads=[R_pb[0]], writes=[res("gmbc")])

    with ExitStack() as ph:
        w1slab = [sb_in(ph, "w1slabA", [128, D]), sb_in(ph, "w1slabB", [128, D])]
        qstage = sb_in(ph, "qstage", [128, D])
        othq = [sb_in(ph, "othqA", [128, D], BF16), sb_in(ph, "othqB", [128, D], BF16)]
        otlq = [sb_in(ph, "otlqA", [128, D], BF16), sb_in(ph, "otlqB", [128, D], BF16)]
        w2slab = [sb_in(ph, "w2slabA", [128, D]), sb_in(ph, "w2slabB", [128, D])]
        w2fold = [sb_in(ph, "w2foldA", [128, 1024]), sb_in(ph, "w2foldB", [128, 1024])]
        oh2 = [sb_in(ph, "oh2A", [128, 1024], BF16), sb_in(ph, "oh2B", [128, 1024], BF16)]
        ol2 = [sb_in(ph, "ol2A", [128, 1024], BF16), sb_in(ph, "ol2B", [128, 1024], BF16)]
        for sl in range(32):
            bb = sl % 2
            dma("sync", w1slab[bb][:], w1[sl * 128:(sl + 1) * 128, :],
                reads=[], writes=[res(f"w1slab{bb}")])
            q4 = sl % 4
            for j in range(8):
                P.op("tensor",
                     lambda e, j=j, bb=bb: e.matmul(pb[j % 2][0:32, :], r128[:],
                                                    w1slab[bb][:, j * 512:(j + 1) * 512],
                                                    start=True, stop=True),
                     reads=[res("r128"), res(f"w1slab{bb}")], writes=[R_pb[j % 2]])
                P.op("vector",
                     lambda e, j=j, q4=q4: e.tensor_copy(
                         qstage[q4 * 32:(q4 + 1) * 32, j * 512:(j + 1) * 512],
                         pb[j % 2][0:32, :]),
                     reads=[R_pb[j % 2]], writes=[res(f"qst_{q4}_{j}")])
            if q4 == 3:
                oc = sl // 4
                qb_ = oc % 2
                for dc in range(32):
                    tb = 2 + ((oc * 32 + dc) % 4)
                    P.op("tensor", lambda e, dc=dc, b=tb: e.transpose(
                        pb[b][:, 0:128], qstage[:, dc * 128:(dc + 1) * 128], id128[:]),
                        reads=[res(f"qst_{q_}_{dc // 4}") for q_ in range(4)]
                              + [res("id128")],
                        writes=[R_pb[tb]])
                    P.op("scalar", lambda e, dc=dc, b=tb, qb_=qb_: e.activation(
                        othq[qb_][:, dc * 128:(dc + 1) * 128], pb[b][:, 0:128],
                        AF.Copy, bias=0.0, scale=1.0),
                        reads=[R_pb[tb]], writes=[res(f"othq{qb_}_{dc}")])
                    P.op("vector", lambda e, dc=dc, b=tb, qb_=qb_: e.scalar_tensor_tensor(
                        otlq[qb_][:, dc * 128:(dc + 1) * 128], pb[b][:, 0:128], 1.0,
                        othq[qb_][:, dc * 128:(dc + 1) * 128],
                        op0=ALU.mult, op1=ALU.subtract),
                        reads=[R_pb[tb], res(f"othq{qb_}_{dc}")],
                        writes=[res(f"otlq{qb_}_{dc}")])
                for h2_, ob, pre in ((0, othq, "othq"), (1, otlq, "otlq")):
                    dma("gpsimd",
                        bass.AP(wt1_scr.tensor, h2_ * 128 * 24 * D + oc * D,
                                [[24 * D, 128], [1, D]]),
                        ob[qb_][:],
                        reads=[res(f"{pre}{qb_}_{dc}") for dc in range(32)],
                        writes=[res(f"wtoq_{oc}_{h2_}")])

            dma("sync", w2slab[bb][:], w2[sl * 128:(sl + 1) * 128, :],
                reads=[], writes=[res(f"w2slab{bb}")])
            w2r = w2slab[bb][:].rearrange("p (c r) -> p c r", r=4)
            P.op("vector", lambda e, w2r=w2r, bb=bb: e.tensor_add(
                w2fold[bb][:].unsqueeze(2), w2r[:, :, 0:1], w2r[:, :, 1:2]),
                reads=[res(f"w2slab{bb}")], writes=[res(f"w2fold{bb}")])
            P.op("vector", lambda e, w2r=w2r, bb=bb: e.tensor_add(
                w2fold[bb][:].unsqueeze(2), w2fold[bb][:].unsqueeze(2), w2r[:, :, 2:3]),
                reads=[res(f"w2slab{bb}")], writes=[res(f"w2fold{bb}")])
            P.op("vector", lambda e, w2r=w2r, bb=bb: e.tensor_add(
                w2fold[bb][:].unsqueeze(2), w2fold[bb][:].unsqueeze(2), w2r[:, :, 3:4]),
                reads=[res(f"w2slab{bb}")], writes=[res(f"w2fold{bb}")])
            for cc in range(8):
                tb = 2 + (cc % 4)
                P.op("tensor", lambda e, cc=cc, bb=bb, b=tb: e.transpose(
                    pb[b][:, 0:128], w2fold[bb][:, cc * 128:(cc + 1) * 128], id128[:]),
                    reads=[res(f"w2fold{bb}"), res("id128")], writes=[R_pb[tb]])
                P.op("scalar", lambda e, cc=cc, bb=bb, b=tb: e.activation(
                    oh2[bb][:, cc * 128:(cc + 1) * 128], pb[b][:, 0:128],
                    AF.Copy, bias=0.0, scale=1.0),
                    reads=[R_pb[tb]], writes=[res(f"oh2{bb}_{cc}")])
                P.op("vector", lambda e, cc=cc, bb=bb, b=tb: e.scalar_tensor_tensor(
                    ol2[bb][:, cc * 128:(cc + 1) * 128], pb[b][:, 0:128], 1.0,
                    oh2[bb][:, cc * 128:(cc + 1) * 128],
                    op0=ALU.mult, op1=ALU.subtract),
                    reads=[R_pb[tb], res(f"oh2{bb}_{cc}")], writes=[res(f"ol2{bb}_{cc}")])
            for h2_, ob, rn in ((0, oh2, "oh2"), (1, ol2, "ol2")):
                dma("gpsimd",
                    bass.AP(wt2_scr.tensor, h2_ * 128 * 8 * D + sl * 128,
                            [[8 * D, 128], [D, 8], [1, 128]]),
                    ob[bb][:].rearrange("p (a b) -> p a b", b=128),
                    reads=[res(f"{rn}{bb}_{cc}") for cc in range(8)],
                    writes=[res(f"wt2o_{sl}_{h2_}")])

        with ExitStack() as pp:
            psl = [sb_in(pp, "pslA", [128, D]), sb_in(pp, "pslB", [128, D])]
            oth = [sb_in(pp, "othA", [128, D], BF16), sb_in(pp, "othB", [128, D], BF16)]
            otl = [sb_in(pp, "otlA", [128, D], BF16), sb_in(pp, "otlB", [128, D], BF16)]

            jobs = []
            for oc in range(8, 24):
                jobs.append((w1[4096 + (oc - 8) * 128: 4096 + (oc - 7) * 128, :],
                             [], wt1_scr, oc, 24))
            for oc in range(32):
                jobs.append((wf[oc * 128:(oc + 1) * 128, :], [], wtf_scr, oc, 32))

            for ji, (srcap, rd, dscr, oc, noc) in enumerate(jobs):
                jb = ji % 2
                dma("sync", psl[jb][:], srcap,
                    reads=rd, writes=[res(f"psl{jb}")])
                for dc in range(32):
                    tb = (ji * 32 + dc) % 6
                    P.op("tensor", lambda e, dc=dc, jb=jb, b=tb: e.transpose(
                        pb[b][:, 0:128], psl[jb][:, dc * 128:(dc + 1) * 128], id128[:]),
                        reads=[res(f"psl{jb}"), res("id128")], writes=[R_pb[tb]])
                    P.op("scalar", lambda e, dc=dc, jb=jb, b=tb: e.activation(
                        oth[jb][:, dc * 128:(dc + 1) * 128], pb[b][:, 0:128],
                        AF.Copy, bias=0.0, scale=1.0),
                        reads=[R_pb[tb]], writes=[res(f"oth{jb}_{dc}")])
                    P.op("vector", lambda e, dc=dc, jb=jb, b=tb: e.scalar_tensor_tensor(
                        otl[jb][:, dc * 128:(dc + 1) * 128], pb[b][:, 0:128], 1.0,
                        oth[jb][:, dc * 128:(dc + 1) * 128],
                        op0=ALU.mult, op1=ALU.subtract),
                        reads=[R_pb[tb], res(f"oth{jb}_{dc}")], writes=[res(f"otl{jb}_{dc}")])
                for h2_, pre in ((0, "oth"), (1, "otl")):
                    dma("gpsimd",
                        bass.AP(dscr.tensor, h2_ * 128 * noc * D + oc * D,
                                [[noc * D, 128], [1, D]]),
                        (oth if h2_ == 0 else otl)[jb][:],
                        reads=[res(f"{pre}{jb}_{dc}") for dc in range(32)],
                        writes=[res(f"wto_{ji}_{h2_}")])
    P.barrier()

    st_qo = ExitStack()
    qbh = sb_in(st_qo, "qbh", [128, 8, OWN], BF16)
    qbl = sb_in(st_qo, "qbl", [128, 8, OWN], BF16)

    with ExitStack() as ph:
        xsb = [sb_in(ph, "xsbA", [128, D]), sb_in(ph, "xsbB", [128, D])]
        lnxTh = sb_in(ph, "lnxTh", [128, 32, 512], BF16)
        lnxTl = sb_in(ph, "lnxTl", [128, 32, 512], BF16)
        stat = [sb_in(ph, "statA", [128, 6]), sb_in(ph, "statB", [128, 6])]
        wtTh = [sb_in(ph, "wtThA", [128, 2048], BF16), sb_in(ph, "wtThB", [128, 2048], BF16)]
        wtTl = [sb_in(ph, "wtTlA", [128, 2048], BF16), sb_in(ph, "wtTlB", [128, 2048], BF16)]
        kth = sb_in(ph, "kth", [128, 8, 512], BF16)
        ktl = sb_in(ph, "ktl", [128, 8, 512], BF16)
        vstage = sb_in(ph, "vstage", [128, 512])
        vtile = sb_in(ph, "vtile", [128, 4, NH * 33])
        kflat = kth[:].rearrange("p a b -> p (a b)")

        for tt in range(4):
            for sl in range(4):
                xb_ = (tt * 4 + sl) % 2
                dma("sync", xsb[xb_][:], x[tt * 512 + sl * 128: tt * 512 + (sl + 1) * 128, :],
                    reads=[], writes=[res(f"xsb{xb_}")])
                st_ = stat[xb_]
                sk = f"stat{xb_}"
                P.op("vector", lambda e, xb_=xb_, st_=st_: e.reduce_sum(
                    st_[:, 0:1], xsb[xb_][:], axis=X),
                     reads=[res(f"xsb{xb_}")], writes=[res(sk)])
                P.op("scalar", lambda e, xb_=xb_, st_=st_: e.activation(
                    kflat[:, 0:4096], xsb[xb_][:],
                    AF.Square, bias=0.0, scale=1.0, accum_out=st_[:, 1:2]),
                     reads=[res(f"xsb{xb_}")],
                     writes=[res(sk)] + [res(f"kth_{g}") for g in range(8)])
                P.op("vector", lambda e, st_=st_: e.tensor_scalar(st_[:, 0:1], st_[:, 0:1],
                                                         1.0 / D, None, op0=ALU.mult),
                     reads=[], writes=[res(sk)])
                P.op("vector", lambda e, st_=st_: e.tensor_tensor(st_[:, 5:6], st_[:, 0:1],
                                                         st_[:, 0:1], op=ALU.mult),
                     reads=[], writes=[res(sk)])
                P.op("vector", lambda e, st_=st_: e.scalar_tensor_tensor(
                    st_[:, 2:3], st_[:, 1:2], 1.0 / D, st_[:, 5:6],
                    op0=ALU.mult, op1=ALU.subtract),
                    reads=[], writes=[res(sk)])
                P.op("scalar", lambda e, st_=st_: e.activation(st_[:, 3:4], st_[:, 2:3], AF.Sqrt,
                                                      bias=float(EPS), scale=1.0),
                     reads=[res(sk)], writes=[res(sk)])
                P.op("vector", lambda e, st_=st_: e.reciprocal(st_[:, 3:4], st_[:, 3:4]),
                     reads=[], writes=[res(sk)])
                P.op("vector", lambda e, st_=st_: e.tensor_tensor(st_[:, 4:5], st_[:, 0:1],
                                                         st_[:, 3:4], op=ALU.mult),
                     reads=[], writes=[res(sk)])
                P.op("vector", lambda e, st_=st_: e.tensor_scalar(st_[:, 4:5], st_[:, 4:5],
                                                         -1.0, None, op0=ALU.mult),
                     reads=[], writes=[res(sk)])
                P.op("scalar", lambda e, xb_=xb_, st_=st_: e.activation(
                    xsb[xb_][:], xsb[xb_][:], AF.Identity,
                    bias=st_[:, 4:5], scale=st_[:, 3:4]),
                     reads=[res(sk)], writes=[res(f"xsb{xb_}")])

                for dc in range(32):
                    bk = dc % 4
                    P.op("tensor", lambda e, dc=dc, b=bk, xb_=xb_: e.transpose(
                        pb[b][:, 0:128], xsb[xb_][:, dc * 128:(dc + 1) * 128], id128[:]),
                        reads=[res(f"xsb{xb_}"), res("id128")], writes=[R_pb[bk]])
                    P.op("scalar", lambda e, dc=dc, sl=sl, b=bk: e.activation(
                        lnxTh[:, dc, sl * 128:(sl + 1) * 128], pb[b][:, 0:128],
                        AF.Copy, bias=0.0, scale=1.0),
                        reads=[R_pb[bk]], writes=[res(f"lnxTh_{dc}_{sl}")])
                    P.op("vector", lambda e, dc=dc, sl=sl, b=bk: e.scalar_tensor_tensor(
                        lnxTl[:, dc, sl * 128:(sl + 1) * 128], pb[b][:, 0:128], 1.0,
                        lnxTh[:, dc, sl * 128:(sl + 1) * 128],
                        op0=ALU.mult, op1=ALU.subtract),
                        reads=[R_pb[bk], res(f"lnxTh_{dc}_{sl}")], writes=[res(f"lnxTl_{dc}_{sl}")])
            ocs = list(range(24)) if tt < 2 else list(range(8, 24))

            def wt1ap(oc, h2_, hh):
                return bass.AP(wt1_scr.tensor, h2_ * 128 * 24 * D + oc * D + hh * 2048,
                               [[24 * D, 128], [1, 2048]])

            def fetch_half(j, halves):
                oc, hh = halves[j]
                b = j % 2
                dma("sync", wtTh[b][:], wt1ap(oc, 0, hh),
                    reads=[], writes=[res(f"wtTh{b}")])
                dma("sync", wtTl[b][:], wt1ap(oc, 1, hh),
                    reads=[], writes=[res(f"wtTl{b}")])

            def mm1(j, dc):
                b = j % 2
                c0 = (dc % 16) * 128
                for pi, (lw, rx, rl, rr_) in enumerate((
                        (wtTh, lnxTh, f"wtTh{b}", "lnxTh"),
                        (wtTh, lnxTl, f"wtTh{b}", "lnxTl"),
                        (wtTl, lnxTh, f"wtTl{b}", "lnxTh"))):
                    P.op("tensor", lambda e, dc=dc, b=b, c0=c0, lw=lw, rx=rx: e.matmul(
                        pb[4][:, 0:512], lw[b][:, c0:c0 + 128], rx[:, dc, :],
                        start=(dc == 0 and lw is wtTh and rx is lnxTh),
                        stop=(dc == 31 and lw is wtTl)),
                        reads=[res(rl)] + [res(f"{rr_}_{dc}_{s_}") for s_ in range(4)],
                        writes=[R_pb[4]])

            def evac_oc(oc, tt):
                if oc < 8:
                    hid = qbh[:, oc, tt * 512:(tt + 1) * 512]
                    lod = qbl[:, oc, tt * 512:(tt + 1) * 512]
                    hr, lr = f"qbh_{oc}_{tt}", f"qbl_{oc}_{tt}"
                elif oc < 16:
                    hid = kth[:, oc - 8, :]
                    lod = ktl[:, oc - 8, :]
                    hr, lr = f"kth_{oc - 8}", f"ktl_{oc - 8}"
                else:
                    hid = lod = None
                if hid is not None:
                    P.op("vector", lambda e, oc=oc, hid=hid: e.tensor_scalar(
                        hid, pb[4][:, 0:512], b1f[:, oc:oc + 1], None, op0=ALU.add),
                        reads=[R_pb[4], res("b1f")], writes=[res(hr)])
                    P.op("vector", lambda e, oc=oc, hid=hid, lod=lod: e.scalar_tensor_tensor(
                        lod, pb[4][:, 0:512], b1f[:, oc:oc + 1], hid,
                        op0=ALU.add, op1=ALU.subtract),
                        reads=[R_pb[4], res("b1f"), res(hr)], writes=[res(lr)])
                    if 8 <= oc < 16:
                        g = oc - 8
                        P.op("vector", lambda e, oc=oc, hid=hid: e.scalar_tensor_tensor(
                            vstage[:], pb[4][:, 0:512], b1f[:, oc:oc + 1], hid,
                            op0=ALU.add, op1=ALU.mult),
                            reads=[R_pb[4], res("b1f"), res(hr)], writes=[res("vstage")])
                        P.op("tensor", lambda e: e.matmul(pb[5][0:4, 0:512], h4[:], vstage[:],
                                                          start=True, stop=True),
                             reads=[res("h4"), res("vstage")], writes=[R_pb[5]])
                        P.op("vector", lambda e: e.reduce_max(nk4t[0:4, 0:1],
                                                              pb[5][0:4, 0:512], axis=X),
                             reads=[R_pb[5]], writes=[res("nk4t")])
                        if tt == 0:
                            P.op("vector", lambda e, g=g: e.tensor_copy(
                                nk48[0:4, g:g + 1], nk4t[0:4, 0:1]),
                                reads=[res("nk4t")], writes=[res("nk48")])
                        else:
                            P.op("vector", lambda e, g=g: e.tensor_tensor(
                                nk48[0:4, g:g + 1], nk48[0:4, g:g + 1], nk4t[0:4, 0:1],
                                op=ALU.max),
                                reads=[res("nk4t")], writes=[res("nk48")])
                else:
                    g = oc - 16
                    P.op("vector", lambda e, oc=oc: e.tensor_scalar(
                        vstage[:], pb[4][:, 0:512], b1f[:, oc:oc + 1], None, op0=ALU.add),
                        reads=[R_pb[4], res("b1f")], writes=[res("vstage")])
                    for sub in range(4):
                        P.op("tensor", lambda e, sub=sub: e.transpose(
                            pb[5][:, 0:128], vstage[:, sub * 128:(sub + 1) * 128], id128[:]),
                            reads=[res("vstage"), res("id128")], writes=[R_pb[5]])
                        dstv = vtile[:].rearrange("p a (h ee) -> p a h ee", ee=33)[
                            :, sub, 4 * g:4 * g + 4, 0:32]
                        srcv = pb[5][:, 0:128].rearrange("p (h ee) -> p h ee", ee=32)
                        P.op("vector", lambda e, d=dstv, s=srcv: e.tensor_copy(d, s),
                             reads=[R_pb[5]], writes=[res(f"vtile_{g}_{sub}")])

            halves = [(oc, hh) for oc in ocs for hh in (0, 1)]
            fetch_half(0, halves)
            fetch_half(1, halves)
            steps = [(oc, dc) for oc in ocs for dc in range(32)]
            for si, (oc, dc) in enumerate(steps):
                mm1(si // 16, dc)
                if dc % 16 == 15 and si // 16 + 2 < len(halves):
                    fetch_half(si // 16 + 2, halves)
                if dc == 31:
                    evac_oc(oc, tt)
            vones = vtile[:].rearrange("p a (h ee) -> p a h ee", ee=33)[:, :, :, 32:33]
            P.op("vector", lambda e, v=vones: e.memset(v, 1.0), writes=[res("vones")])
            KSP = 128 * 8 * TS
            dma("gpsimd", bass.AP(kspill.tensor, tt * 512, [[8 * TS, 128], [TS, 8], [1, 512]]),
                kth[:], reads=[res(f"kth_{g}") for g in range(8)],
                writes=[res(f"kspill_h_{tt}")])
            dma("gpsimd", bass.AP(kspill.tensor, KSP + tt * 512, [[8 * TS, 128], [TS, 8], [1, 512]]),
                ktl[:], reads=[res(f"ktl_{g}") for g in range(8)],
                writes=[res(f"kspill_l_{tt}")])
            dma("gpsimd",
                bass.AP(vspill.tensor, tt * 4 * (NH * 33), [[16 * NH * 33, 128], [1, 4 * NH * 33]]),
                vtile[:].rearrange("p a b -> p (a b)"),
                reads=[res(f"vtile_{g}_{sub}") for g in range(8) for sub in range(4)]
                      + [res("vones")],
                writes=[res(f"vspill_{tt}")])
        for g in range(8):
            for t4 in range(2):
                P.op("vector", lambda e, g=g, t=t4: e.tensor_tensor(
                    vstage[:], qbh[:, g, t * 512:(t + 1) * 512],
                    qbl[:, g, t * 512:(t + 1) * 512], op=ALU.add),
                    reads=[res(f"qbh_{g}_{t4}"), res(f"qbl_{g}_{t4}")], writes=[res("vstage")])
                P.op("vector", lambda e: e.tensor_tensor(
                    vstage[:], vstage[:], vstage[:], op=ALU.mult),
                    reads=[], writes=[res("vstage")])
                P.op("tensor", lambda e, t=t4: e.matmul(
                    pa[0:4, t * 512:(t + 1) * 512], h4[:], vstage[:],
                    start=True, stop=True),
                    reads=[res("h4"), res("vstage")], writes=[R_pa[t4]])
            P.op("vector", lambda e: e.reduce_max(vstage[0:4, 0:1], pa[0:4, 0:1024], axis=X),
                 reads=[R_pa[0], R_pa[1]], writes=[res("vstage")])
            P.op("tensor", lambda e: e.transpose(pb[5][0:1, 0:4], vstage[0:4, 0:1],
                                                 id128[0:4, 0:4]),
                 reads=[res("vstage"), res("id128")], writes=[R_pb[5]])
            P.op("vector", lambda e, g=g: e.tensor_copy(nqrow[0:1, 4 * g:4 * g + 4],
                                                        pb[5][0:1, 0:4]),
                 reads=[R_pb[5]], writes=[res("nqrow")])
        for g in range(8):
            P.op("tensor", lambda e, g=g: e.transpose(pb[5][0:1, 0:4], nk48[0:4, g:g + 1],
                                                      id128[0:4, 0:4]),
                 reads=[res("nk48"), res("id128")], writes=[R_pb[5]])
            P.op("vector", lambda e, g=g: e.tensor_copy(nkrow[0:1, 4 * g:4 * g + 4],
                                                        pb[5][0:1, 0:4]),
                 reads=[R_pb[5]], writes=[res("nkrow")])
    P.barrier()

    o32h = sb_in(st_qo, "o32h", [128, 8, OWN], BF16)
    o32l = sb_in(st_qo, "o32l", [128, 8, OWN], BF16)

    with ExitStack() as ph:
        khead = [sb_in(ph, "kheadA", [128, TS], BF16), sb_in(ph, "kheadB", [128, TS], BF16)]
        qhead = [sb_in(ph, "qheadA", [128, OWN], BF16), sb_in(ph, "qheadB", [128, OWN], BF16)]
        vg = [sb_in(ph, "vgA", [128, 16, 132]), sb_in(ph, "vgB", [128, 16, 132])]
        ptile = [sb_in(ph, "ptA", [128, 512]), sb_in(ph, "ptB", [128, 512]),
                 sb_in(ph, "ptC", [128, 512])]
        tmp32 = sb_in(ph, "tmp32", [128, 512])
        zbcs = sb_in(ph, "zbcs", [32, 512])
        zr = sb_in(ph, "zr", [1, 1024])
        knm = [sb_in(ph, "knmA", [4, 4]), sb_in(ph, "knmB", [4, 4])]

        KSP = 128 * 8 * TS

        def fetch_head(h, buf):
            g, rrf = h // 4, h % 4
            dma("sync", khead[buf][0:32, :],
                bass.AP(kspill.tensor, 32 * rrf * 8 * TS + g * TS, [[8 * TS, 32], [1, TS]]),
                reads=[], writes=[res(f"khead{buf}")])
            dma("sync", khead[buf][32:64, :],
                bass.AP(kspill.tensor, KSP + 32 * rrf * 8 * TS + g * TS, [[8 * TS, 32], [1, TS]]),
                reads=[], writes=[res(f"khead{buf}")])
            dma("sync", khead[buf][64:96, :],
                bass.AP(kspill.tensor, 32 * rrf * 8 * TS + g * TS, [[8 * TS, 32], [1, TS]]),
                reads=[], writes=[res(f"khead{buf}")])
            qrd = [res(f"qbh_{g}_0"), res(f"qbh_{g}_1")]
            qrdl = [res(f"qbl_{g}_0"), res(f"qbl_{g}_1")]
            dma("sync", qhead[buf][0:32, :], qbh[32 * rrf:32 * rrf + 32, g, :],
                reads=qrd, writes=[res(f"qhead{buf}")])
            dma("sync", qhead[buf][32:64, :], qbh[32 * rrf:32 * rrf + 32, g, :],
                reads=qrd, writes=[res(f"qhead{buf}")])
            dma("sync", qhead[buf][64:96, :], qbl[32 * rrf:32 * rrf + 32, g, :],
                reads=qrdl, writes=[res(f"qhead{buf}")])

        def prep_group(g):
            gb_ = g % 2
            km = knm[gb_]
            kk = f"knm{gb_}"
            dma("sync", vg[gb_][:],
                bass.AP(vspill.tensor, g * 132, [[16 * NH * 33, 128], [NH * 33, 16], [1, 132]]),
                reads=[], writes=[res(f"vg{gb_}")])
            P.op("vector", lambda e, g=g, km=km: e.tensor_tensor(
                km[0:1, 0:4], nqrow[0:1, 4 * g:4 * g + 4], nkrow[0:1, 4 * g:4 * g + 4],
                op=ALU.mult),
                reads=[res("nqrow"), res("nkrow")], writes=[res(kk)])
            P.op("scalar", lambda e, km=km: e.activation(km[0:1, 0:4], km[0:1, 0:4], AF.Sqrt,
                                                  bias=0.0, scale=1.0),
                 reads=[res(kk)], writes=[res(kk)])
            P.op("vector", lambda e, km=km: e.tensor_scalar(km[0:1, 0:4], km[0:1, 0:4],
                                                     -SCALE, -1.0, op0=ALU.mult, op1=ALU.add),
                 reads=[], writes=[res(kk)])
            P.op("tensor", lambda e, km=km: e.matmul(pb[5][:, 0:4], ones1[:], km[0:1, 0:4],
                                              start=True, stop=True),
                 reads=[res("ones1"), res(kk)], writes=[R_pb[5]])
            P.op("vector", lambda e, g=g: e.tensor_copy(negm[:, 4 * g:4 * g + 4],
                                                        pb[5][:, 0:4]),
                 reads=[R_pb[5]], writes=[res(f"negm_{g}")])

        fetch_head(0, 0)
        prep_group(0)
        fin_prev = [None]
        for g in range(8):
            gb_ = g % 2

            for rr in range(4):
                h = 4 * g + rr
                hb = h % 2
                if h + 1 < 32:
                    fetch_head(h + 1, 1 - hb)
                iters = []
                for j in range(4):
                    iters.append(("diag", 0, j, None))
                for kc in (8, 9, 10, 11):
                    iters.append(("full", 0, kc, 0))
                for j in range(4):
                    iters.append(("diag", 1, j, None))
                for kc in (0, 1, 2, 3, 8, 9, 10, 11):
                    iters.append(("full", 1, kc, None))
                for kc in (12, 13, 14, 15):
                    iters.append(("full", 1, kc, 1))
                first_of_qc = {0: 0, 1: 8}
                last_of_qc = {0: 7, 1: 23}

                def emit_qk(it, it_i):
                    kind, qc, a, cnd = it
                    bs = it_i % 3
                    k0 = (qc * 512 + a * 128) if kind == "diag" else a * 128
                    P.op("tensor", lambda e, k0=k0, qc=qc, b=bs, hb=hb: e.matmul(
                        pb[b][:, 0:512], khead[hb][0:96, k0:k0 + 128],
                        qhead[hb][0:96, qc * 512:(qc + 1) * 512],
                        start=True, stop=True),
                        reads=[res(f"khead{hb}"), res(f"qhead{hb}")], writes=[R_pb[bs]])

                def emit_rest(it, it_i):
                    kind, qc, a, cnd = it
                    bs = it_i % 3
                    pv = it_i % 3
                    P.op("scalar", lambda e, b=bs, p=pv, h=h: e.activation(
                        ptile[p][:], pb[b][:, 0:512], AF.Exp,
                        bias=negm[:, h:h + 1], scale=SCALE),
                        reads=[R_pb[bs], res(f"negm_{g}")], writes=[res(f"pt{pv}")])
                    if kind == "diag":
                        P.op("vector", lambda e, p=pv, j=a: e.tensor_tensor(
                            ptile[p][:], ptile[p][:],
                            t01[:, 384 - 128 * j:896 - 128 * j], op=ALU.mult),
                            reads=[res("t01")], writes=[res(f"pt{pv}")])
                    elif cnd is not None:
                        P.op("vector", lambda e, p=pv, c=cnd: e.tensor_scalar(
                            ptile[p][:], ptile[p][:], gmbc[:, c:c + 1], None,
                            op0=ALU.mult),
                            reads=[res("gmbc")], writes=[res(f"pt{pv}")])
                    kc = (qc * 4 + a) if kind == "diag" else a
                    oa = pa[0:33, qc * 512:(qc + 1) * 512]
                    sa = (it_i == first_of_qc[qc])
                    so = (it_i == last_of_qc[qc])
                    off = 132 * kc + 33 * rr
                    P.op("tensor", lambda e, p=pv, oa=oa, sa=sa, so=so, off=off, gb_=gb_: e.matmul(
                        oa, vg[gb_][:].rearrange("p a b -> p (a b)")[:, off:off + 33],
                        ptile[p][:], start=sa, stop=so),
                        reads=[res(f"vg{gb_}"), res(f"pt{pv}")], writes=[R_pa[qc]])

                emit_qk(iters[0], 0)
                emit_qk(iters[1], 1)
                if fin_prev[0] is not None:
                    fin_prev[0]()
                    fin_prev[0] = None
                if rr == 1 and g + 1 < 8:
                    prep_group(g + 1)
                for it_i in range(len(iters)):
                    if it_i + 2 < len(iters):
                        emit_qk(iters[it_i + 2], it_i + 2)
                    emit_rest(iters[it_i], it_i)
                def fin(g=g, rr=rr):
                    P.op("vector", lambda e: e.reciprocal(zr[0:1, :], pa[32:33, 0:1024]),
                         reads=[R_pa[0], R_pa[1]], writes=[res("zr")])
                    for qc2 in range(2):
                        zb = 3 + qc2
                        P.op("tensor", lambda e, qc2=qc2, zb=zb: e.matmul(
                            pb[zb][0:32, 0:512], ones1[0:1, 0:32],
                            zr[0:1, qc2 * 512:(qc2 + 1) * 512], start=True, stop=True),
                            reads=[res("ones1"), res("zr")], writes=[R_pb[zb]])
                        P.op("scalar", lambda e, zb=zb: e.activation(
                            zbcs[:], pb[zb][0:32, 0:512], AF.Copy, bias=0.0, scale=1.0),
                            reads=[R_pb[zb]], writes=[res("zbcs")])
                        P.op("vector", lambda e, qc2=qc2, rr=rr: e.tensor_tensor(
                            tmp32[32 * rr:32 * rr + 32, :],
                            pa[0:32, qc2 * 512:(qc2 + 1) * 512],
                            zbcs[:], op=ALU.mult),
                            reads=[R_pa[qc2], res("zbcs")], writes=[res("tmp32")])
                        hk = [res(f"o32h_{g}_{rr}_{q8_}")
                              for q8_ in range(qc2 * 4, qc2 * 4 + 4)]
                        lk = [res(f"o32l_{g}_{rr}_{q8_}")
                              for q8_ in range(qc2 * 4, qc2 * 4 + 4)]
                        P.op("vector", lambda e, qc2=qc2, rr=rr, g=g: e.tensor_copy(
                            o32h[32 * rr:32 * rr + 32, g, qc2 * 512:(qc2 + 1) * 512],
                            tmp32[32 * rr:32 * rr + 32, :]),
                            reads=[res("tmp32")], writes=hk)
                        P.op("vector", lambda e, qc2=qc2, rr=rr, g=g: e.scalar_tensor_tensor(
                            o32l[32 * rr:32 * rr + 32, g, qc2 * 512:(qc2 + 1) * 512],
                            tmp32[32 * rr:32 * rr + 32, :], 1.0,
                            o32h[32 * rr:32 * rr + 32, g, qc2 * 512:(qc2 + 1) * 512],
                            op0=ALU.mult, op1=ALU.subtract),
                            reads=[res("tmp32")] + hk, writes=lk)
                fin_prev[0] = fin
        fin_prev[0]()
    P.barrier()

    with ExitStack() as ph:
        b2st = sb_in(ph, "b2st", [1, 512])
        rhh = [sb_in(ph, "rhhA", [128, 512], BF16), sb_in(ph, "rhhB", [128, 512], BF16)]
        rhl = [sb_in(ph, "rhlA", [128, 512], BF16), sb_in(ph, "rhlB", [128, 512], BF16)]
        y1sw = sb_in(ph, "y1sw", [128, 4, D])
        xst = [sb_in(ph, "xstA", [128, 512]), sb_in(ph, "xstB", [128, 512])]
        st3 = sb_in(ph, "st3", [128, 2])


        def fetch_rh(ot, cc):
            rb = cc % 2
            for h2_, dst, rn in ((0, rhh, "rhh"), (1, rhl, "rhl")):
                dma("sync", dst[rb][:],
                    bass.AP(wt2_scr.tensor, h2_ * 128 * 8 * D + cc * D + ot * 512,
                            [[8 * D, 128], [1, 512]]),
                    reads=[], writes=[res(f"{rn}{rb}")])

        fetch_rh(0, 0)
        for sweep in range(2):
            for ot in range(8):
                dma("sync", b2st[:], b2[None, ot * 512:(ot + 1) * 512],
                    reads=[], writes=[res("b2st")])
                for cc in range(8):
                    rb = cc % 2
                    if cc + 1 < 8:
                        fetch_rh(ot, cc + 1)
                    elif ot + 1 < 8:
                        fetch_rh(ot + 1, 0)
                    elif sweep == 0:
                        fetch_rh(0, 0)
                    for qq in range(4):
                        qb = sweep * 4 + qq
                        trip = ((o32h, rhh, "o32h", f"rhh{rb}", cc == 0),
                                (o32h, rhl, "o32h", f"rhl{rb}", False),
                                (o32l, rhh, "o32l", f"rhh{rb}", False))
                        for (lo_, rx, lrn, rrn, sa_) in trip:
                            P.op("tensor", lambda e, cc=cc, qb=qb, qq=qq, rb=rb,
                                 lo_=lo_, rx=rx, sa_=sa_: e.matmul(
                                pb[qq][:, 0:512], lo_[:, cc, qb * 128:(qb + 1) * 128],
                                rx[rb][:], start=sa_, stop=False),
                                reads=[res(f"{lrn}_{cc}_{rr_}_{qb}") for rr_ in range(4)]
                                      + [res(rrn)],
                                writes=[R_pb[qq]])
                for qq in range(4):
                    qb = sweep * 4 + qq
                    xb = qq % 2
                    P.op("tensor", lambda e, qq=qq: e.matmul(
                        pb[qq][:, 0:512], ones1[:], b2st[:],
                        start=False, stop=True),
                        reads=[res("ones1"), res("b2st")], writes=[R_pb[qq]])
                    dma("sync", xst[xb][:], x[qb * 128:(qb + 1) * 128, ot * 512:(ot + 1) * 512],
                        reads=[], writes=[res(f"xst{xb}")])
                    P.op("vector", lambda e, qq=qq, ot=ot, xb=xb: e.tensor_tensor(
                        y1sw[:, qq, ot * 512:(ot + 1) * 512], pb[qq][:, 0:512],
                        xst[xb][:], op=ALU.add),
                        reads=[R_pb[qq], res(f"xst{xb}")], writes=[res(f"y1sw_{qq}")])
            for qq in range(4):
                qb = sweep * 4 + qq
                P.op("vector", lambda e, qq=qq: e.reduce_sum(st3[:, 0:1],
                                                             y1sw[:, qq, :], axis=X),
                     reads=[res(f"y1sw_{qq}")], writes=[res("st3")])
                P.op("vector", lambda e, qb=qb: e.tensor_scalar(
                    ln2m[:, qb:qb + 1], st3[:, 0:1], 1.0 / D, None, op0=ALU.mult),
                    reads=[res("st3")], writes=[res("ln2m")])
                dma("gpsimd", y1scr[qb], y1sw[:, qq, :],
                    reads=[res(f"y1sw_{qq}")], writes=[res(f"y1scr_{qb}")])
    P.barrier()
    st_qo.close()

    st_h = ExitStack()
    h2h = sb_in(st_h, "h2h", [128, 32, OWN], BF16)
    h2l = sb_in(st_h, "h2l", [128, 32, OWN], BF16)
    umT = sb_in(st_h, "umT", [64, OWN])
    btsb = sb_in(st_h, "btsb", [64, D])
    with ExitStack() as ph:
        agh = sb_in(ph, "agh", [128, 32, 68], BF16)
        agl = sb_in(ph, "agl", [128, 32, 68], BF16)
        with ExitStack() as pp:
            lasb = sb_in(pp, "lasb", [64, D])
            gwsb = sb_in(pp, "gwsb", [4, D])
            bsl = sb_in(pp, "bsl", [128, 64])
            dma("sync", lasb[:], la, writes=[res("lasb")])
            dma("sync", gwsb[:], gw, writes=[res("gwsb")])
            for dc in range(32):
                b6 = 2 + (dc % 2)
                P.op("tensor", lambda e, dc=dc, b=b6: e.transpose(
                    pb[b][:, 0:64], lasb[0:64, dc * 128:(dc + 1) * 128],
                    id128[0:64, 0:64]),
                    reads=[res("lasb"), res("id128")], writes=[R_pb[b6]])
                P.op("vector", lambda e, dc=dc, b=b6: e.tensor_copy(
                    agh[:, dc, 0:64], pb[b][:, 0:64]),
                    reads=[R_pb[b6]], writes=[res(f"agh_{dc}")])
                P.op("vector", lambda e, dc=dc, b=b6: e.scalar_tensor_tensor(
                    agl[:, dc, 0:64], pb[b][:, 0:64], 1.0, agh[:, dc, 0:64],
                    op0=ALU.mult, op1=ALU.subtract),
                    reads=[R_pb[b6], res(f"agh_{dc}")], writes=[res(f"agl_{dc}")])
                P.op("tensor", lambda e, dc=dc, b=b6: e.transpose(
                    pb[b][:, 64:68], gwsb[0:4, dc * 128:(dc + 1) * 128],
                    id128[0:4, 0:4]),
                    reads=[res("gwsb"), res("id128")], writes=[R_pb[b6]])
                P.op("vector", lambda e, dc=dc, b=b6: e.tensor_copy(
                    agh[:, dc, 64:68], pb[b][:, 64:68]),
                    reads=[R_pb[b6]], writes=[res(f"agh_{dc}")])
                P.op("vector", lambda e, dc=dc, b=b6: e.scalar_tensor_tensor(
                    agl[:, dc, 64:68], pb[b][:, 64:68], 1.0, agh[:, dc, 64:68],
                    op0=ALU.mult, op1=ALU.subtract),
                    reads=[R_pb[b6], res(f"agh_{dc}")], writes=[res(f"agl_{dc}")])
            for sl in range(32):
                dma("sync", bsl[:], lb[sl * 128:(sl + 1) * 128, :],
                    reads=[], writes=[res("bsl")])
                b6 = 2 + (sl % 2)
                P.op("tensor", lambda e, b=b6: e.transpose(pb[b][0:64, 0:128],
                                                           bsl[:], id128[:]),
                     reads=[res("bsl"), res("id128")], writes=[R_pb[b6]])
                P.op("vector", lambda e, sl=sl, b=b6: e.tensor_copy(
                    btsb[0:64, sl * 128:(sl + 1) * 128], pb[b][0:64, 0:128]),
                    reads=[R_pb[b6]], writes=[res(f"btsb_{sl}")])
        P.barrier()

        y1sl = [sb_in(ph, "y1slA", [128, D]), sb_in(ph, "y1slB", [128, D])]
        st4 = [sb_in(ph, "st4A", [128, 6]), sb_in(ph, "st4B", [128, 6])]
        usb = sb_in(ph, "usb", [128, 68])
        msk = sb_in(ph, "msk", [128, 64])
        sg = sb_in(ph, "sg", [128, 8])
        for qb in range(8):
            yb = qb % 2
            dma("sync", y1sl[yb][:], y1scr[qb], reads=[], writes=[res(f"y1sl{yb}")])
            dum = h2h[:, :, qb * 128:(qb + 1) * 128]
            s4 = st4[yb]
            s4k = f"st4{yb}"
            P.op("scalar", lambda e, d=dum, yb=yb, s4=s4: e.activation(d, y1sl[yb][:], AF.Square,
                                                                bias=0.0, scale=1.0,
                                                                accum_out=s4[:, 1:2]),
                 reads=[res(f"y1sl{yb}")],
                 writes=[res(s4k)] + [res(f"h2h_{dc}_{qb}") for dc in range(32)])
            P.op("vector", lambda e, qb=qb, s4=s4: e.tensor_tensor(
                s4[:, 5:6], ln2m[:, qb:qb + 1], ln2m[:, qb:qb + 1], op=ALU.mult),
                reads=[res("ln2m")], writes=[res(s4k)])
            P.op("vector", lambda e, s4=s4: e.scalar_tensor_tensor(
                s4[:, 2:3], s4[:, 1:2], 1.0 / D, s4[:, 5:6],
                op0=ALU.mult, op1=ALU.subtract),
                reads=[], writes=[res(s4k)])
            P.op("scalar", lambda e, s4=s4: e.activation(s4[:, 3:4], s4[:, 2:3], AF.Sqrt,
                                                  bias=float(EPS), scale=1.0),
                 reads=[res(s4k)], writes=[res(s4k)])
            P.op("vector", lambda e, s4=s4: e.reciprocal(s4[:, 3:4], s4[:, 3:4]),
                 reads=[], writes=[res(s4k)])
            P.op("vector", lambda e, qb=qb, s4=s4: e.tensor_tensor(
                s4[:, 4:5], ln2m[:, qb:qb + 1], s4[:, 3:4], op=ALU.mult),
                reads=[res("ln2m")], writes=[res(s4k)])
            P.op("vector", lambda e, s4=s4: e.tensor_scalar(s4[:, 4:5], s4[:, 4:5], -1.0,
                                                     None, op0=ALU.mult),
                 reads=[], writes=[res(s4k)])
            P.op("scalar", lambda e, yb=yb, s4=s4: e.activation(y1sl[yb][:], y1sl[yb][:],
                                                  AF.Identity,
                                                  bias=s4[:, 4:5], scale=s4[:, 3:4]),
                 reads=[res(s4k)], writes=[res(f"y1sl{yb}")])
            for dc in range(32):
                b6 = dc % 4
                P.op("tensor", lambda e, dc=dc, b=b6, yb=yb: e.transpose(
                    pb[b][:, 0:128], y1sl[yb][:, dc * 128:(dc + 1) * 128], id128[:]),
                    reads=[res(f"y1sl{yb}"), res("id128")], writes=[R_pb[b6]])
                P.op("scalar", lambda e, dc=dc, qb=qb, b=b6: e.activation(
                    h2h[:, dc, qb * 128:(qb + 1) * 128], pb[b][:, 0:128],
                    AF.Copy, bias=0.0, scale=1.0),
                    reads=[R_pb[b6]], writes=[res(f"h2h_{dc}_{qb}")])
                P.op("vector", lambda e, dc=dc, qb=qb, b=b6: e.scalar_tensor_tensor(
                    h2l[:, dc, qb * 128:(qb + 1) * 128], pb[b][:, 0:128], 1.0,
                    h2h[:, dc, qb * 128:(qb + 1) * 128],
                    op0=ALU.mult, op1=ALU.subtract),
                    reads=[R_pb[b6], res(f"h2h_{dc}_{qb}")], writes=[res(f"h2l_{dc}_{qb}")])
            for dc in range(32):
                trip = ((h2h, agh, "h2h", "agh", dc == 0, False),
                        (h2h, agl, "h2h", "agl", False, False),
                        (h2l, agh, "h2l", "agl", False, dc == 31))
                for (lx, rx, lrn, rrn, sa_, so_) in trip:
                    P.op("tensor", lambda e, dc=dc, qb=qb, lx=lx, rx=rx, sa_=sa_, so_=so_: e.matmul(
                        pb[4][:, 0:68], lx[:, dc, qb * 128:(qb + 1) * 128],
                        rx[:, dc, :], start=sa_, stop=so_),
                        reads=[res(f"{lrn}_{dc}_{qb}"), res(f"agh_{dc}"), res(f"agl_{dc}")],
                        writes=[R_pb[4]])
            P.op("vector", lambda e: e.tensor_copy(usb[:, 0:64], pb[4][:, 0:64]),
                 reads=[R_pb[4]], writes=[res("usb")])
            P.op("vector", lambda e: e.tensor_tensor(usb[:, 64:68], pb[4][:, 64:68],
                                                     gbbc[:], op=ALU.add),
                 reads=[R_pb[4], res("gbbc")], writes=[res("usb")])
            s_ = [usb[:, 64 + i:65 + i] for i in range(4)]
            P.op("vector", lambda e: e.tensor_tensor(sg[:, 0:1], s_[2], s_[3], op=ALU.max),
                 reads=[res("usb")], writes=[res("sg")])
            P.op("vector", lambda e: e.tensor_tensor(sg[:, 1:2], sg[:, 0:1], s_[1], op=ALU.max),
                 reads=[res("usb")], writes=[res("sg")])
            P.op("vector", lambda e: e.tensor_tensor(sg[:, 2:3], sg[:, 1:2], s_[0], op=ALU.is_gt),
                 reads=[res("usb")], writes=[res("sg")])
            P.op("vector", lambda e: e.tensor_tensor(sg[:, 3:4], s_[0], s_[1], op=ALU.max),
                 reads=[res("usb")], writes=[res("sg")])
            P.op("vector", lambda e: e.tensor_tensor(sg[:, 4:5], sg[:, 0:1], sg[:, 3:4], op=ALU.is_gt),
                 reads=[], writes=[res("sg")])
            P.op("vector", lambda e: e.tensor_tensor(sg[:, 5:6], sg[:, 3:4], s_[2], op=ALU.max),
                 reads=[res("usb")], writes=[res("sg")])
            P.op("vector", lambda e: e.tensor_tensor(sg[:, 6:7], s_[3], sg[:, 5:6], op=ALU.is_gt),
                 reads=[res("usb")], writes=[res("sg")])
            P.op("vector", lambda e: e.tensor_scalar(sg[:, 7:8], sg[:, 2:3], 8.0, 8.0,
                                                     op0=ALU.mult, op1=ALU.add),
                 reads=[], writes=[res("sg")])
            P.op("vector", lambda e: e.scalar_tensor_tensor(sg[:, 7:8], sg[:, 4:5], 16.0,
                                                            sg[:, 7:8], op0=ALU.mult, op1=ALU.add),
                 reads=[], writes=[res("sg")])
            P.op("vector", lambda e: e.scalar_tensor_tensor(sg[:, 7:8], sg[:, 6:7], 32.0,
                                                            sg[:, 7:8], op0=ALU.mult, op1=ALU.add),
                 reads=[], writes=[res("sg")])
            P.op("vector", lambda e: e.tensor_scalar(msk[:], iota64[:], sg[:, 7:8], None,
                                                     op0=ALU.is_lt),
                 reads=[res("iota64"), res("sg")], writes=[res("msk")])
            P.op("vector", lambda e: e.tensor_tensor(msk[:], msk[:], usb[:, 0:64], op=ALU.mult),
                 reads=[res("usb")], writes=[res("msk")])
            P.op("tensor", lambda e: e.transpose(pb[2][0:64, 0:128], msk[:], id128[:]),
                 reads=[res("msk"), res("id128")], writes=[R_pb[2]])
            P.op("vector", lambda e, qb=qb: e.tensor_copy(
                umT[0:64, qb * 128:(qb + 1) * 128], pb[2][0:64, 0:128]),
                reads=[R_pb[2]], writes=[res(f"umT_{qb}")])
    P.barrier()

    with ExitStack() as ph:
        wT5h = [sb_in(ph, "wT5hA", [128, D], BF16), sb_in(ph, "wT5hB", [128, D], BF16)]
        wT5l = [sb_in(ph, "wT5lA", [128, D], BF16), sb_in(ph, "wT5lB", [128, D], BF16)]
        y2fm = [sb_in(ph, "y2fmA", [128, 1024]), sb_in(ph, "y2fmB", [128, 1024])]
        y2tm = [sb_in(ph, "y2tmA", [128, 128]), sb_in(ph, "y2tmB", [128, 128]),
                sb_in(ph, "y2tmC", [128, 128]), sb_in(ph, "y2tmD", [128, 128])]
        y1t = [[sb_in(ph, f"y1t{p}_{i}", [128, 128]) for i in range(8)] for p in range(2)]

        def fetch_y1(oc):
            par = oc % 2
            for qb in range(8):
                dma("sync", y1t[par][qb][:],
                    bass.AP(y1scr.tensor, qb * 128 * D + oc * 128, [[D, 128], [1, 128]]),
                    reads=[], writes=[res(f"y1t_{par}_{qb}")])

        def fetch5(oc):
            b = oc % 2
            for h2_, dst, rn in ((0, wT5h, "wT5h"), (1, wT5l, "wT5l")):
                dma("sync", dst[b][:],
                    bass.AP(wtf_scr.tensor, h2_ * 128 * 32 * D + oc * D,
                            [[32 * D, 128], [1, D]]),
                    reads=[], writes=[res(f"{rn}{b}")])

        def mm5(oc, dc):
            b = oc % 2
            c0 = dc * 128
            for qt in range(2):
                trip = ((wT5h, h2h, f"wT5h{b}", "h2h", dc == 0),
                        (wT5h, h2l, f"wT5h{b}", "h2l", False),
                        (wT5l, h2h, f"wT5l{b}", "h2h", False))
                for (lw, rx, lrn, rrn, sa_) in trip:
                    P.op("tensor", lambda e, dc=dc, qt=qt, b=b, c0=c0, lw=lw, rx=rx,
                         sa_=sa_: e.matmul(
                        pb[qt][:, 0:512], lw[b][:, c0:c0 + 128],
                        rx[:, dc, qt * 512:(qt + 1) * 512],
                        start=sa_, stop=False),
                        reads=[res(lrn)] + [res(f"{rrn}_{dc}_{qb_}")
                                            for qb_ in range(4 * qt, 4 * qt + 4)],
                        writes=[R_pb[qt]])

        def finish5(oc):
            po = oc % 2
            for qt in range(2):
                P.op("tensor", lambda e, oc=oc, qt=qt: e.matmul(
                    pb[qt][:, 0:512], btsb[0:64, oc * 128:(oc + 1) * 128],
                    umT[0:64, qt * 512:(qt + 1) * 512], start=False, stop=True),
                    reads=[res(f"btsb_{oc}")] + [res(f"umT_{qb_}")
                                                 for qb_ in range(4 * qt, 4 * qt + 4)],
                    writes=[R_pb[qt]])
                P.op("vector", lambda e, oc=oc, qt=qt, po=po: e.tensor_scalar(
                    y2fm[po][:, qt * 512:(qt + 1) * 512], pb[qt][:, 0:512],
                    bsb[:, oc:oc + 1], None, op0=ALU.add),
                    reads=[R_pb[qt], res("bsb")], writes=[res(f"y2fm{po}")])
            if oc + 1 < 32:
                fetch_y1(oc + 1)
            par = oc % 2
            for qb in range(8):
                b6 = 2 + (qb % 4)
                pt2 = qb % 4
                P.op("tensor", lambda e, qb=qb, b=b6, po=po: e.transpose(
                    pb[b][:, 0:128], y2fm[po][:, qb * 128:(qb + 1) * 128], id128[:]),
                    reads=[res(f"y2fm{po}"), res("id128")], writes=[R_pb[b6]])
                P.op("vector", lambda e, b=b6, pt2=pt2, par=par, qb=qb: e.tensor_tensor(
                    y2tm[pt2][:], pb[b][:, 0:128], y1t[par][qb][:], op=ALU.add),
                     reads=[R_pb[b6], res(f"y1t_{par}_{qb}")], writes=[res(f"y2tm{pt2}")])
                dma("gpsimd",
                    bass.AP(y.tensor, qb * 128 * D + oc * 128, [[D, 128], [1, 128]]),
                    y2tm[pt2][:], reads=[res(f"y2tm{pt2}")], writes=[res(f"yout_{oc}_{qb}")])

        fetch5(0)
        fetch5(1)
        fetch_y1(0)
        for oc in range(32):
            for dc in range(32):
                mm5(oc, dc)
            if oc + 2 < 32:
                fetch5(oc + 2)
            finish5(oc)
    st_h.close()

    final_lanes = [(l, P.cnt[l]) for l in LANES if l.startswith("gpsimd") and P.cnt[l] > 0]
    with nc.Block() as block:
        def mk(engname):
            def body(eng):
                hi_waited = {}
                for fn, waits, key in P.ops[engname]:
                    for (o, c) in waits:
                        if c > hi_waited.get(o, 0):
                            eng.wait_ge(sems[o], c)
                            hi_waited[o] = c
                    if fn == "waits_only":
                        pass
                    elif key is None:
                        fn(eng)
                    else:
                        fn(eng).then_inc(sems[key], 16 if key not in ENGS else 1)
                if engname == "gpsimd":
                    for (l, c) in final_lanes:
                        eng.wait_ge(sems[l], c)
            return body
        block.sync(mk("sync"))
        block.scalar(mk("scalar"))
        block.vector(mk("vector"))
        block.tensor(mk("tensor"))
        block.gpsimd(mk("gpsimd"))

    top.close()
    return nc


_PERM_CHUNKS = {0: [0, 3, 1, 2], 1: [1, 2, 0, 3]}


def _make_sched(parity):
    return np.array([[1.0 if parity == 1 else 0.0,
                      1.0 if parity == 0 else 0.0]], dtype=np.float32)


def kernel(L0_attn_x, L0_attn_W1, L0_attn_b1, L0_attn_W2, L0_attn_b2,
           L0_ffn_W, L0_ffn_b, L0_ffn_A, L0_ffn_B, L0_ffn_gate_w, L0_ffn_gate_b):
    xf = np.ascontiguousarray(np.asarray(L0_attn_x, dtype=np.float32))
    w1 = np.ascontiguousarray(np.asarray(L0_attn_W1, dtype=np.float32))
    b1v = np.ascontiguousarray(np.asarray(L0_attn_b1, dtype=np.float32))
    w2 = np.ascontiguousarray(np.asarray(L0_attn_W2, dtype=np.float32))
    b2v = np.ascontiguousarray(np.asarray(L0_attn_b2, dtype=np.float32))
    wfv = np.ascontiguousarray(np.asarray(L0_ffn_W, dtype=np.float32))
    bfv = np.ascontiguousarray(np.asarray(L0_ffn_b, dtype=np.float32))
    lav = np.ascontiguousarray(np.asarray(L0_ffn_A, dtype=np.float32))
    lbv = np.ascontiguousarray(np.asarray(L0_ffn_B, dtype=np.float32))
    gwv = np.ascontiguousarray(np.asarray(L0_ffn_gate_w, dtype=np.float32))
    gbv = np.ascontiguousarray(np.asarray(L0_ffn_gate_b, dtype=np.float32))

    nc = build_program()
    in_maps = []
    for c in range(8):
        s, p = c // 2, c % 2
        perm = np.concatenate([np.arange(g * 512, (g + 1) * 512) for g in _PERM_CHUNKS[p]])
        in_maps.append({
            "x": np.ascontiguousarray(xf[s][perm]),
            "w1": w1, "b1": b1v, "w2": w2, "b2": b2v,
            "wf": wfv, "bf": bfv, "la": lav, "lb": lbv, "gw": gwv, "gb": gbv,
            "sched": _make_sched(p),
        })
    resu = run_bass_kernel_spmd(nc, in_maps, core_ids=list(range(8)))
    out = np.empty((4, TS, D), dtype=np.float32)
    for c in range(8):
        s, p = c // 2, c % 2
        shard = np.asarray(resu.results[c]["y"], dtype=np.float32)
        for i, g in enumerate(_PERM_CHUNKS[p][:2]):
            out[s, g * 512:(g + 1) * 512, :] = shard[i * 512:(i + 1) * 512, :]
    return out
